# version 02
# baseline (speedup 1.0000x reference)
from contextlib import ExitStack
import numpy as np
import ml_dtypes
import concourse.bass as bass
import concourse.mybir as mybir
from concourse.bass_utils import run_bass_kernel_spmd

F32 = mybir.dt.float32
BF16 = mybir.dt.bfloat16
AF = mybir.ActivationFunctionType
ALU = mybir.AluOpType
AX = mybir.AxisListType.X

P = 128
D = 2048
KT = D // P
E = 8
FT = 2048 // P
EPS = 1e-5
NEG_BIG = -1e30
NCORES = 8


def _wblocks(Wmat):
    M = Wmat.shape[0] // P
    K = Wmat.shape[1] // P
    t = Wmat.reshape(M, P, K, P)
    return np.ascontiguousarray(t.transpose(0, 3, 2, 1)).reshape(M, P, K * P)


class _Eng:
    def __init__(self):
        self.thunks = []
        self.waited = {}

    def wait(self, ev):
        if ev is None:
            return
        sem, v = ev
        if v <= 0 or self.waited.get(id(sem), 0) >= v:
            return
        self.waited[id(sem)] = v
        self.thunks.append(("w", (sem, v)))

    def replay(self, q):
        for kind, x in self.thunks:
            if kind == "w":
                q.wait_ge(x[0], x[1])
            else:
                x(q)


def build_nc(T_core, W, C, dbg=False, stop_after=None):
    assert T_core % C == 0 and W <= C
    NSZ = min(512, C)
    NS = C // NSZ
    JT = C // P
    T_all = W + T_core
    NWB = 2

    nc = bass.Bass("TRN2", target_bir_lowering=False)
    ap = lambda t: t.ap()

    x_h = ap(nc.dram_tensor("x_fm", [D, T_all], F32, kind="ExternalInput"))
    wi_h = ap(nc.dram_tensor("wi_blk", [KT, P, D], F32, kind="ExternalInput"))
    wf_h = ap(nc.dram_tensor("wf_blk", [KT, P, D], F32, kind="ExternalInput"))
    wg_h = ap(nc.dram_tensor("wg_blk", [KT, P, D], F32, kind="ExternalInput"))
    wo_h = ap(nc.dram_tensor("wo_blk", [KT, P, D], F32, kind="ExternalInput"))
    w1_h = ap(nc.dram_tensor("w1_blk", [E, FT, P, D], F32, kind="ExternalInput"))
    w3_h = ap(nc.dram_tensor("w3_blk", [E, FT, P, D], F32, kind="ExternalInput"))
    w2_h = ap(nc.dram_tensor("w2_blk", [E, KT, P, 2048], F32, kind="ExternalInput"))
    gw_h = ap(nc.dram_tensor("gw_t", [KT, P, E], F32, kind="ExternalInput"))
    gn_h = ap(nc.dram_tensor("gn_t", [P, KT], F32, kind="ExternalInput"))
    idt_h = ap(nc.dram_tensor("idt", [P, P], F32, kind="ExternalInput"))
    idtb_h = ap(nc.dram_tensor("idtb", [P, P], F32, kind="ExternalInput"))
    oh_h = ap(nc.dram_tensor("oh", [E, E * P], F32, kind="ExternalInput"))
    on1_h = ap(nc.dram_tensor("on1", [1, P], F32, kind="ExternalInput"))
    onb_h = ap(nc.dram_tensor("onb", [P, 1], F32, kind="ExternalInput"))
    eps_h = ap(nc.dram_tensor("epsc", [1, 1], F32, kind="ExternalInput"))
    out_h = ap(nc.dram_tensor("out_fm", [D, T_core], F32, kind="ExternalOutput"))
    if dbg:
        dbg_o = ap(nc.dram_tensor("dbg_o", [P, KT * C], F32, kind="ExternalOutput"))
        dbg_v = ap(nc.dram_tensor("dbg_v", [P, KT * C // 2], F32, kind="ExternalOutput"))
        dbg_gate = ap(nc.dram_tensor("dbg_gate", [P, 8 * C], F32, kind="ExternalOutput"))
        dbg_o2 = ap(nc.dram_tensor("dbg_o2", [P, KT * C], F32, kind="ExternalOutput"))
        dbg_y = ap(nc.dram_tensor("dbg_y", [P, KT * C], F32, kind="ExternalOutput"))
        dbg_x1 = ap(nc.dram_tensor("dbg_x1", [P, KT * C], F32, kind="ExternalOutput"))
        dbg_w8 = ap(nc.dram_tensor("dbg_w8", [E, C], F32, kind="ExternalOutput"))
        dbg_y2 = ap(nc.dram_tensor("dbg_y2", [P, KT * C], F32, kind="ExternalOutput"))
        dbg_st = ap(nc.dram_tensor("dbg_st", [P, KT], F32, kind="ExternalOutput"))
        dbg_lt = ap(nc.dram_tensor("dbg_lt", [P, (C // P) * E], F32, kind="ExternalOutput"))
        dbg_et = ap(nc.dram_tensor("dbg_et", [P, (C // P) * E], F32, kind="ExternalOutput"))
        dbg_den = ap(nc.dram_tensor("dbg_den", [P, C // P], F32, kind="ExternalOutput"))
        dbg_msk = ap(nc.dram_tensor("dbg_msk", [P, (C // P) * E], F32, kind="ExternalOutput"))

    with ExitStack() as _stack:
        en = _stack.enter_context
        GV = en(nc.sbuf_tensor("GV", [P, KT * C], F32))
        T2 = en(nc.sbuf_tensor("T2", [P, KT * C // 2], F32))
        XB = en(nc.sbuf_tensor("XB", [P, KT * C], BF16))
        SG = en(nc.sbuf_tensor("SG", [P, KT * C], BF16))
        EPH = en(nc.sbuf_tensor("EPH", [P, 6 * C], BF16))
        WBUF = en(nc.sbuf_tensor("WBUF", [P, NWB * D], BF16))
        BC = en(nc.sbuf_tensor("BC", [P, 2 * C], F32))
        AB = en(nc.sbuf_tensor("AB", [1, 2 * NSZ], F32))
        MRow = en(nc.sbuf_tensor("MRow", [1, NSZ], BF16))
        WBC = en(nc.sbuf_tensor("WBC", [P, C], BF16))
        L8 = en(nc.sbuf_tensor("L8", [E, C], BF16))
        W8 = en(nc.sbuf_tensor("W8", [E, C], BF16))
        LT = en(nc.sbuf_tensor("LT", [P, JT * E], BF16))
        ET = en(nc.sbuf_tensor("ET", [P, JT * E], F32))
        MSK = en(nc.sbuf_tensor("MSK", [P, JT * E], F32))
        M1 = en(nc.sbuf_tensor("M1", [P, JT], F32))
        M1N = en(nc.sbuf_tensor("M1N", [P, JT], F32))
        M2 = en(nc.sbuf_tensor("M2", [P, JT], F32))
        DEN = en(nc.sbuf_tensor("DEN", [P, JT], F32))
        STATE = en(nc.sbuf_tensor("STATE", [P, KT], F32))
        GN = en(nc.sbuf_tensor("GN", [P, KT], F32))
        IDTB = en(nc.sbuf_tensor("IDTB", [P, P], BF16))
        OH = en(nc.sbuf_tensor("OH", [E, E * P], BF16))
        ON1 = en(nc.sbuf_tensor("ON1", [1, P], F32))
        ONB = en(nc.sbuf_tensor("ONB", [P, 1], BF16))
        GWT = en(nc.sbuf_tensor("GWT", [P, KT * E], BF16))
        EPSC = en(nc.sbuf_tensor("EPSC", [1, 1], F32))
        PA = en(nc.psum_tensor("PA", [P, 1024], F32))
        PF = en(nc.psum_tensor("PF", [P, 1024], F32))
        PB = en(nc.psum_tensor("PB", [P, 1024], F32))
        PC = en(nc.psum_tensor("PC", [P, 512], F32))
        PD = en(nc.psum_tensor("PD", [P, 512], F32))
        s_pe = en(nc.semaphore("s_pe"))
        s_dv = en(nc.semaphore("s_dv"))
        s_ac = en(nc.semaphore("s_ac"))
        s_wt0 = en(nc.semaphore("s_wt0"))
        s_wt1 = en(nc.semaphore("s_wt1"))
        s_xb = en(nc.semaphore("s_xb"))
        s_out = en(nc.semaphore("s_out"))
        s_x = en(nc.semaphore("s_x"))
        s_cn = en(nc.semaphore("s_cn"))
        block = en(nc.Block())
        GVb = GV.bitcast(BF16)
        T2b = T2.bitcast(BF16)
        XBf = XB.bitcast(F32)
        PDb = PD.bitcast(BF16)

        te, dv, ac, gp, sy = _Eng(), _Eng(), _Eng(), _Eng(), _Eng()
        cnt = {"pe": 0, "dv": 0, "ac": 0, "wt0": 0, "wt1": 0, "xb": 0, "out": 0, "x": 0, "cn": 0}
        sems = {"pe": s_pe, "dv": s_dv, "ac": s_ac, "wt0": s_wt0, "wt1": s_wt1,
                "xb": s_xb, "out": s_out, "x": s_x, "cn": s_cn}
        engs = {"pe": te, "dv": dv, "ac": ac}
        step = {"pe": 1, "dv": 1, "ac": 1, "wt0": 16, "wt1": 16, "xb": 16, "out": 16, "x": 16, "cn": 16}

        def emit(which, fn, mark=False, eng=None):
            eng = eng if eng is not None else engs[which]
            if not mark:
                eng.thunks.append(("i", fn))
                return None
            sem = sems[which]
            st = step[which]
            def g(q, fn=fn, sem=sem, st=st):
                fn(q).then_inc(sem, st)
            eng.thunks.append(("i", g))
            cnt[which] += st
            return (sem, cnt[which])

        E_te = lambda fn, mark=False: emit("pe", fn, mark)

        def dvs(fn):
            ev = emit("dv", fn, True)
            dv.wait(ev)
            return ev
        E_dv = lambda fn, mark=False: emit("dv", fn, mark)
        E_ac = lambda fn, mark=False: emit("ac", fn, mark)
        E_gp = lambda fn, cls: emit(cls, fn, mark=True, eng=gp)
        E_sy = lambda fn, cls: emit(cls, fn, mark=True, eng=sy)

        def gate_t(mi):
            return GV[:, (8 + mi) * C:(9 + mi) * C]

        def o_t(k):
            return GVb[:, k * C:(k + 1) * C]

        def y_t(m):
            return GV[:, m * C:(m + 1) * C]

        def v_t(mi):
            return T2b[:, mi * C:(mi + 1) * C]

        def o2_t(k):
            return T2b[:, k * C:(k + 1) * C]

        def hB_t(k):
            return T2b[:, k * C:(k + 1) * C]

        def hA_t(k):
            return SG[:, k * C:(k + 1) * C]

        def sg_t(k):
            return SG[:, k * C:(k + 1) * C]

        def xb_t(k):
            return XB[:, k * C:(k + 1) * C]

        def eph_t(i):
            return EPH[:, i * C:(i + 1) * C]

        def out_t(m):
            return XBf[:, m * C:(m + 1) * C] if m < 8 else T2[:, (m - 8) * C:(m - 7) * C]

        wslot_free = [None] * NWB
        wslot_ld = [None] * NWB
        wb_i = [0]

        def dump(dst_ap, src_ap, evs):
            for e_ in evs:
                gp.wait(e_)
            return E_gp(lambda q, d=dst_ap, s=src_ap: q.dma_start(out=d, in_=s), "xb")

        def wload(src):
            s = wb_i[0] % NWB
            gp.wait(wslot_free[s])
            dst = WBUF[:, s * D:(s + 1) * D]
            ev = E_gp(lambda q, dst=dst, src=src: q.dma_start(out=dst, in_=src),
                      "wt0" if s == 0 else "wt1")
            wslot_ld[s] = ev
            wb_i[0] += 1
            return dst, s

        def mm_block(blk, rhs_t, ps, Cc, mark_each=False):
            nsz = min(512, Cc)
            nn = Cc // nsz
            last = None
            for k in range(KT):
                for n in range(nn):
                    mark = (k == KT - 1 and n == nn - 1)
                    ev = E_te(lambda q, blk=blk, ps=ps, k=k, n=n, nsz=nsz,
                              st=(k == 0), sp=(k == KT - 1), rhs_t=rhs_t:
                              q.matmul(ps[:, n * nsz:(n + 1) * nsz],
                                       blk[:, k * P:(k + 1) * P],
                                       rhs_t(k)[:, n * nsz:(n + 1) * nsz],
                                       start=st, stop=sp),
                              mark=mark)
                    if mark:
                        last = ev
            return last

        pfree = {"PA": None, "PF": None, "PB": None, "PC": None, "PD": None}
        ev_gate_free = [None] * 8
        ev_gv_free = None
        ev_sg_free = None
        ev_t2_dma = None
        ev_hfree = [None, None]
        ev_eph34 = [None, None]
        ev_eph012 = [None, None, None]
        ev_state0 = None

        ev_const = None
        for dst, src in ((GN[:], gn_h), (ON1[:], on1_h),
                         (OH[:], oh_h), (ONB[:], onb_h), (IDTB[:], idtb_h),
                         (GWT[:].rearrange("p (k e) -> p k e", k=KT),
                          gw_h.rearrange("k p e -> p k e")),
                         (EPSC[:], eps_h)):
            ev_const = E_gp(lambda q, d=dst, s=src: q.dma_start(out=d, in_=s), "cn")
        ev_state0 = E_dv(lambda q: q.memset(STATE[:], 0.0), mark=True)

        x_kpt = x_h.rearrange("(k p) t -> p k t", p=P)

        nchunks = T_core // C
        chunks = [("warm", 0, W)] + [("real", W + i * C, C) for i in range(nchunks)]
        ev_y2 = [None] * KT

        for ci, (kind, tok0, Cc) in enumerate(chunks):
            warm = kind == "warm"
            nsz = min(512, Cc)
            nn = Cc // nsz
            jt = Cc // P

            gp.wait(ev_const)
            ev_xb = E_gp(lambda q, t0=tok0, cc=Cc:
                         q.dma_start(out=XB[:].rearrange("p (k c) -> p k c", k=KT)[:, :, 0:cc],
                                     in_=x_kpt[:, :, t0:t0 + cc]),
                         "xb")

            mats = [("i", wi_h), ("f", wf_h)] if warm else [("i", wi_h), ("f", wf_h), ("g", wg_h)]
            ev_scan = [None] * KT
            ev_sg = [None] * KT
            ev_o2 = [None] * KT
            for half in range(2):
                for mi in range(8):
                    m = half * 8 + mi
                    ev_mm = {}
                    for mat, hsrc in mats:
                        blk, slot = wload(hsrc[m])
                        ps = {"i": PA, "f": PF, "g": PB}[mat]
                        te.wait(wslot_ld[slot])
                        te.wait(ev_xb)
                        te.wait(pfree["PA" if mat == "i" else "PF" if mat == "f" else "PB"])
                        last = mm_block(blk, lambda k: xb_t(k), ps, Cc)
                        wslot_free[slot] = last
                        ev_mm[mat] = last
                    ac.wait(ev_mm["f"])
                    ac.wait(ev_gate_free[mi])
                    ac.wait(ev_gv_free)
                    ac.wait(ev_eph012[0])
                    ac.wait(ev_eph012[1])
                    ev_g1 = E_ac(lambda q, mi=mi, cc=Cc:
                                 q.activation(gate_t(mi)[:, 0:cc], PF[:, 0:cc], AF.Sigmoid),
                                 mark=True)
                    ev_e0 = E_ac(lambda q, cc=Cc:
                                 q.activation(eph_t(0)[:, 0:cc], PF[:, 0:cc], AF.Sigmoid, scale=-1.0),
                                 mark=True)
                    pfree["PF"] = ev_e0
                    ac.wait(ev_mm["i"])
                    ev_e1 = E_ac(lambda q, cc=Cc:
                                 q.activation(eph_t(1)[:, 0:cc], PA[:, 0:cc], AF.Silu),
                                 mark=True)
                    pfree["PA"] = ev_e1
                    dv.wait(ev_e0)
                    dv.wait(ev_e1)
                    dv.wait(ev_t2_dma)
                    ev_v = E_dv(lambda q, mi=mi, cc=Cc:
                                q.tensor_mul(v_t(mi)[:, 0:cc], eph_t(1)[:, 0:cc], eph_t(0)[:, 0:cc]),
                                mark=True)
                    ev_eph012[0] = ev_v
                    ev_eph012[1] = ev_v
                    if not warm:
                        ac.wait(ev_mm["g"])
                        ac.wait(ev_eph012[2])
                        ev_e2 = E_ac(lambda q, cc=Cc:
                                     q.activation(eph_t(2)[:, 0:cc], PB[:, 0:cc], AF.Silu),
                                     mark=True)
                        pfree["PB"] = ev_e2
                        dv.wait(ev_e2)
                        dv.wait(ev_sg_free)
                        dv.wait(ev_hfree[0])
                        ev_sg[m] = E_dv(lambda q, m=m, cc=Cc:
                                        q.tensor_single_scalar(sg_t(m)[:, 0:cc], eph_t(2)[:, 0:cc],
                                                               GN[:, m:m + 1], ALU.mult),
                                        mark=True)
                        ev_eph012[2] = ev_sg[m]
                    dv.wait(ev_g1)
                    dv.wait(ev_state0)
                    dvs(lambda q, m=m, mi=mi, cc=Cc:
                        q.tensor_tensor_scan(o_t(m)[:, 0:cc], gate_t(mi)[:, 0:cc],
                                             v_t(mi)[:, 0:cc], STATE[:, m:m + 1],
                                             ALU.mult, ALU.add))
                    ev_scan[m] = dvs(lambda q, m=m, cc=Cc:
                                     q.tensor_copy(STATE[:, m:m + 1], o_t(m)[:, cc - 1:cc]))
                    ev_gate_free[mi] = ev_scan[m]
            if warm:
                if dbg:
                    ev_d = dump(dbg_st, STATE[:], [ev_scan[KT - 1]])
                    dv.wait(ev_d)
                    if dbg == 2:
                        ev_d = dump(dbg_v, T2b[:, 0:KT * C // 2], [ev_scan[KT - 1]])
                        ev_d2 = dump(dbg_o, GVb[:, 0:KT * C], [ev_scan[KT - 1]])
                        ev_d3 = dump(dbg_gate, GV[:, 8 * C:KT * C], [ev_scan[KT - 1]])
                        dv.wait(ev_d)
                        dv.wait(ev_d2)
                        ac.wait(ev_d3)
                        sy.wait(ev_d3)
                continue
            last_real = (ci == len(chunks) - 1)
            if dbg == 2:
                last_real = False
            if dbg and last_real:
                ev_d = dump(dbg_v, T2b[:, 0:KT * C // 2], [ev_scan[KT - 1]])
                ev_d2 = dump(dbg_o, GVb[:, 0:KT * C], [ev_scan[KT - 1]])
                ev_d3 = dump(dbg_gate, GV[:, 8 * C:KT * C], [ev_scan[KT - 1]])
                dv.wait(ev_d)
                dv.wait(ev_d2)
                ac.wait(ev_d3)
                sy.wait(ev_d3)

            if stop_after == "s1":
                continue
            ev_row = None
            for k in range(KT):
                sl = 3 + (k % 2)
                ac.wait(ev_scan[k])
                ac.wait(ev_eph34[k % 2])
                ev_sq = E_ac(lambda q, k=k, sl=sl, cc=Cc:
                             q.activation(eph_t(sl)[:, 0:cc], o_t(k)[:, 0:cc], AF.Square),
                             mark=True)
                te.wait(ev_sq)
                if k == 0:
                    te.wait(pfree["PC"])
                evk = None
                for n in range(nn):
                    r = 0 if n == 0 else 32
                    evk = E_te(lambda q, sl=sl, n=n, r=r, nsz=nsz, st=(k == 0), sp=(k == KT - 1):
                               q.matmul(PC[r:r + 1, 0:nsz], ONB[:, :],
                                        eph_t(sl)[:, n * nsz:(n + 1) * nsz],
                                        start=st, stop=sp),
                               mark=(n == nn - 1))
                ev_eph34[k % 2] = evk
                if k == KT - 1:
                    ev_row = evk
            ev_bcr = None
            ev_pcr = None
            for n in range(nn):
                r = 0 if n == 0 else 32
                dv.wait(ev_row)
                ev_pcr = E_dv(lambda q, r=r, nsz=nsz:
                              q.tensor_copy(AB[0:1, 0:nsz], PC[r:r + 1, 0:nsz]),
                              mark=(n == nn - 1))
                ev_r = E_dv(lambda q, nsz=nsz: q.tensor_scalar_mul(AB[0:1, 0:nsz], AB[0:1, 0:nsz], 1.0 / D),
                            mark=True)
                ac.wait(ev_r)
                E_ac(lambda q, nsz=nsz: q.activation(AB[0:1, 0:nsz], AB[0:1, 0:nsz], AF.Ln, bias=EPSC[0:1, 0:1]))
                ev_rr = E_ac(lambda q, nsz=nsz:
                             q.activation(AB[0:1, 0:nsz], AB[0:1, 0:nsz], AF.Exp, scale=-0.5),
                             mark=True)
                te.wait(ev_rr)
                te.wait(pfree["PD"])
                ev_t = E_te(lambda q, nsz=nsz:
                            q.matmul(PD[:, 0:nsz], ON1[:, :], AB[0:1, 0:nsz],
                                     start=True, stop=True),
                            mark=True)
                dv.wait(ev_t)
                ev_bcr = E_dv(lambda q, n=n, nsz=nsz:
                              q.tensor_copy(BC[:, n * nsz:(n + 1) * nsz], PD[:, 0:nsz]),
                              mark=True)
                pfree["PD"] = ev_bcr
                ac.wait(ev_t)
                dv.wait(ev_t)
            pfree["PC"] = ev_pcr
            for k in range(KT):
                dv.wait(ev_bcr)
                dv.wait(ev_sg[k])
                E_dv(lambda q, k=k, cc=Cc:
                     q.tensor_mul(eph_t(5)[:, 0:cc], o_t(k)[:, 0:cc], BC[:, 0:cc]))
                ev_o2[k] = E_dv(lambda q, k=k, cc=Cc:
                                q.tensor_mul(o2_t(k)[:, 0:cc], eph_t(5)[:, 0:cc], sg_t(k)[:, 0:cc]),
                                mark=True)

            if dbg and last_real:
                ev_d = dump(dbg_o2, T2b[:, 0:KT * C], [e_ for e_ in ev_o2])
                dv.wait(ev_d)
                te.wait(ev_d)
            if stop_after == "s3":
                continue
            ev_y = [None] * KT
            ev_x1 = [None] * KT
            ev_x1b = [None] * KT
            ev_rowS = None
            for k in range(KT):
                sy.wait(ev_o2[k])
            sy.wait(ev_scan[KT - 1])
            ev_x = E_sy(lambda q, t0=tok0, cc=Cc:
                        q.dma_start(out=GV[:].rearrange("p (k c) -> p k c", k=KT)[:, :, 0:cc],
                                    in_=x_kpt[:, :, t0:t0 + cc]),
                        "x")
            for m in range(KT):
                pname = "PA" if m % 2 == 0 else "PF"
                ps = PA if m % 2 == 0 else PF
                blk, slot = wload(wo_h[m])
                te.wait(wslot_ld[slot])
                te.wait(pfree[pname])
                for k in range(KT):
                    te.wait(ev_o2[k])
                last = mm_block(blk, lambda k: o2_t(k), ps, Cc)
                wslot_free[slot] = last
                dv.wait(last)
                dv.wait(ev_x)
                ev_y[m] = E_dv(lambda q, m=m, ps=ps, cc=Cc:
                               q.tensor_add(y_t(m)[:, 0:cc], ps[:, 0:cc], y_t(m)[:, 0:cc]),
                               mark=True)
                pfree[pname] = ev_y[m]
                ac.wait(ev_y[m])
                ac.wait(ev_o2[m])
                ev_yb = E_ac(lambda q, m=m, cc=Cc:
                             q.copy(sg_t(m)[:, 0:cc], y_t(m)[:, 0:cc]),
                             mark=True)
                sl = 3 + (m % 2)
                ac.wait(ev_eph34[m % 2])
                ev_sqy = E_ac(lambda q, m=m, sl=sl, cc=Cc:
                              q.activation(eph_t(sl)[:, 0:cc], y_t(m)[:, 0:cc], AF.Square),
                              mark=True)
                te.wait(ev_yb)
                te.wait(ev_sqy)
                if m == 0:
                    te.wait(pfree["PC"])
                    te.wait(pfree["PD"])
                evk = None
                for n in range(nn):
                    rS = 0 if n == 0 else 32
                    psQ = PC[64:65, 0:nsz] if n == 0 else PD[0:1, 0:nsz]
                    E_te(lambda q, m=m, n=n, rS=rS, nsz=nsz, st=(m == 0), sp=(m == KT - 1):
                         q.matmul(PC[rS:rS + 1, 0:nsz], ONB[:, :],
                                  sg_t(m)[:, n * nsz:(n + 1) * nsz], start=st, stop=sp))
                    evk = E_te(lambda q, sl=sl, n=n, psQ=psQ, nsz=nsz, st=(m == 0), sp=(m == KT - 1):
                               q.matmul(psQ, ONB[:, :],
                                        eph_t(sl)[:, n * nsz:(n + 1) * nsz], start=st, stop=sp),
                               mark=(n == nn - 1))
                ev_eph34[m % 2] = evk
                if m == KT - 1:
                    ev_rowS = evk

            def ln_rows(ev_rows):
                ev_bc = None
                ev_pc_read = None
                for n in reversed(range(nn)):
                    rS = 0 if n == 0 else 32
                    psQ = PC[64:65, 0:nsz] if n == 0 else PD[0:1, 0:nsz]
                    dv.wait(ev_rows)
                    E_dv(lambda q, psQ=psQ, nsz=nsz:
                         q.tensor_copy(AB[0:1, 0:nsz], psQ))
                    ev_pc_read = E_dv(lambda q, rS=rS, nsz=nsz:
                                      q.tensor_copy(AB[0:1, nsz:2 * nsz],
                                                    PC[rS:rS + 1, 0:nsz]),
                                      mark=True)
                    if n == nn - 1 and nn > 1:
                        pfree["PD"] = ev_pc_read
                    E_dv(lambda q, nsz=nsz:
                         q.tensor_scalar_mul(AB[0:1, 0:nsz], AB[0:1, 0:nsz], 1.0 / D))
                    E_dv(lambda q, nsz=nsz:
                         q.tensor_scalar_mul(AB[0:1, nsz:2 * nsz], AB[0:1, nsz:2 * nsz], 1.0 / D))
                    E_dv(lambda q, nsz=nsz:
                         q.tensor_mul(MRow[0:1, 0:nsz],
                                      AB[0:1, nsz:2 * nsz], AB[0:1, nsz:2 * nsz]))
                    ev_var = E_dv(lambda q, nsz=nsz:
                                  q.tensor_sub(AB[0:1, 0:nsz], AB[0:1, 0:nsz],
                                               MRow[0:1, 0:nsz]),
                                  mark=True)
                    ac.wait(ev_var)
                    E_ac(lambda q, nsz=nsz:
                         q.activation(AB[0:1, 0:nsz], AB[0:1, 0:nsz], AF.Ln, bias=EPSC[0:1, 0:1]))
                    ev_A = E_ac(lambda q, nsz=nsz:
                                q.activation(AB[0:1, 0:nsz], AB[0:1, 0:nsz], AF.Exp, scale=-0.5),
                                mark=True)
                    dv.wait(ev_A)
                    E_dv(lambda q, nsz=nsz:
                         q.tensor_mul(AB[0:1, nsz:2 * nsz], AB[0:1, nsz:2 * nsz],
                                      AB[0:1, 0:nsz]))
                    ev_B = E_dv(lambda q, nsz=nsz:
                                q.tensor_scalar_mul(AB[0:1, nsz:2 * nsz],
                                                    AB[0:1, nsz:2 * nsz], -1.0),
                                mark=True)
                    for off_ab, off_bc, evr in ((0, 0, ev_A), (nsz, C, ev_B)):
                        te.wait(evr)
                        te.wait(pfree["PD"])
                        ev_t = E_te(lambda q, off_ab=off_ab, nsz=nsz:
                                    q.matmul(PD[:, 0:nsz], ON1[:, :],
                                             AB[0:1, off_ab:off_ab + nsz],
                                             start=True, stop=True),
                                    mark=True)
                        dv.wait(ev_t)
                        ev_bc = E_dv(lambda q, off_bc=off_bc, n=n, nsz=nsz:
                                     q.tensor_copy(BC[:, off_bc + n * nsz:off_bc + (n + 1) * nsz],
                                                   PD[:, 0:nsz]),
                                     mark=True)
                        pfree["PD"] = ev_bc
                    ac.wait(ev_t)
                    dv.wait(ev_t)
                return ev_bc, ev_pc_read

            if dbg and last_real:
                ev_d = dump(dbg_y, GV[:], [e_ for e_ in ev_y])
                dv.wait(ev_d)
            ev_bc1, ev_pc1 = ln_rows(ev_rowS)
            pfree["PC"] = ev_pc1
            for m in range(KT):
                dv.wait(ev_bc1)
                E_dv(lambda q, m=m, cc=Cc:
                     q.tensor_mul(y_t(m)[:, 0:cc], y_t(m)[:, 0:cc], BC[:, 0:cc]))
                ev_x1[m] = E_dv(lambda q, m=m, cc=Cc:
                                q.tensor_add(y_t(m)[:, 0:cc], y_t(m)[:, 0:cc], BC[:, C:C + cc]),
                                mark=True)
                ac.wait(ev_x1[m])
                ev_x1b[m] = E_ac(lambda q, m=m, cc=Cc:
                                 q.copy(xb_t(m)[:, 0:cc], y_t(m)[:, 0:cc]),
                                 mark=True)

            if dbg and last_real:
                ev_d = dump(dbg_x1, GV[:], [e_ for e_ in ev_x1])
                dv.wait(ev_d)
                te.wait(ev_d)
            if stop_after == "s4":
                continue
            ev_l8 = None
            te.wait(pfree["PC"])
            for n in range(nn):
                for m in range(KT):
                    te.wait(ev_x1b[m])
                rb = 0 if n == 0 else 32
                last = None
                for k in range(KT):
                    last = E_te(lambda q, k=k, n=n, rb=rb, nsz=nsz, st=(k == 0), sp=(k == KT - 1):
                                q.matmul(PC[rb:rb + E, 0:nsz], GWT[:, k * E:(k + 1) * E],
                                         xb_t(k)[:, n * nsz:(n + 1) * nsz],
                                         start=st, stop=sp),
                                mark=(k == KT - 1))
                dv.wait(last)
                ev_l8 = E_dv(lambda q, n=n, rb=rb, nsz=nsz:
                             q.tensor_copy(L8[:, n * nsz:(n + 1) * nsz], PC[rb:rb + E, 0:nsz]),
                             mark=True)
            pfree["PC"] = ev_l8
            if stop_after == "s4b_scores":
                continue
            ev_w8 = None
            for j in range(jt):
                te.wait(ev_l8)
                te.wait(pfree["PD"])
                ev_t = E_te(lambda q, j=j:
                            q.transpose(PDb[:, 0:E], L8[:, j * P:(j + 1) * P], IDTB[0:E, 0:E]),
                            mark=True)
                dv.wait(ev_t)
                ev_lt = dvs(lambda q, j=j:
                            q.tensor_copy(LT[:, j * E:(j + 1) * E], PDb[:, 0:E]))
                pfree["PD"] = ev_lt
                if stop_after == "s4b_t1":
                    continue
                dvs(lambda q, j=j: q.tensor_reduce(M1[:, j:j + 1], LT[:, j * E:(j + 1) * E], AX, ALU.max))
                dvs(lambda q, j=j: q.tensor_scalar_mul(M1N[:, j:j + 1], M1[:, j:j + 1], -1.0))
                dvs(lambda q, j=j: q.tensor_single_scalar(MSK[:, j * E:(j + 1) * E],
                                                          LT[:, j * E:(j + 1) * E], M1[:, j:j + 1], ALU.is_ge))
                dvs(lambda q, j=j: q.scalar_tensor_tensor(MSK[:, j * E:(j + 1) * E],
                                                          MSK[:, j * E:(j + 1) * E], NEG_BIG,
                                                          LT[:, j * E:(j + 1) * E], ALU.mult, ALU.add))
                dvs(lambda q, j=j: q.tensor_reduce(M2[:, j:j + 1], MSK[:, j * E:(j + 1) * E], AX, ALU.max))
                ev_mk = dvs(lambda q, j=j: q.tensor_single_scalar(MSK[:, j * E:(j + 1) * E],
                                                                  LT[:, j * E:(j + 1) * E],
                                                                  M2[:, j:j + 1], ALU.is_ge))
                if stop_after == "s4b_mask":
                    continue
                if dbg and last_real and j == jt - 1:
                    ev_d = dump(dbg_lt, LT[:], [ev_mk])
                    ev_d2 = dump(dbg_et, MSK[:], [ev_mk])
                    ac.wait(ev_d)
                    ac.wait(ev_d2)
                    dv.wait(ev_d)
                    dv.wait(ev_d2)
                ac.wait(ev_mk)
                ev_ex = E_ac(lambda q, j=j:
                             q.activation(ET[:, j * E:(j + 1) * E], LT[:, j * E:(j + 1) * E], AF.Exp,
                                          bias=M1N[:, j:j + 1], accum_out=DEN[:, j:j + 1]),
                             mark=True)
                dv.wait(ev_ex)
                dvs(lambda q, j=j: q.reciprocal(DEN[:, j:j + 1], DEN[:, j:j + 1]))
                dvs(lambda q, j=j: q.tensor_mul(MSK[:, j * E:(j + 1) * E],
                                                MSK[:, j * E:(j + 1) * E], ET[:, j * E:(j + 1) * E]))
                dvs(lambda q, j=j: q.tensor_single_scalar(MSK[:, j * E:(j + 1) * E],
                                                          MSK[:, j * E:(j + 1) * E],
                                                          DEN[:, j:j + 1], ALU.mult))
                ev_c = dvs(lambda q, j=j: q.tensor_copy(LT[:, j * E:(j + 1) * E],
                                                        MSK[:, j * E:(j + 1) * E]))
                te.wait(ev_c)
                te.wait(pfree["PD"])
                ev_t2_ = E_te(lambda q, j=j:
                              q.transpose(PDb[0:E, 0:P], LT[:, j * E:(j + 1) * E], IDTB[:, :]),
                              mark=True)
                dv.wait(ev_t2_)
                ev_w8 = E_dv(lambda q, j=j:
                             q.tensor_copy(W8[:, j * P:(j + 1) * P], PDb[0:E, 0:P]),
                             mark=True)
                pfree["PD"] = ev_w8
                if dbg and last_real and j == jt - 1:
                    dump(dbg_msk, MSK[:], [ev_w8])

            if dbg and last_real:
                ev_d = dump(dbg_w8, W8[:], [ev_w8])
                te.wait(ev_d)
                dv.wait(ev_d)
                dump(dbg_den, DEN[:], [ev_w8])
            if stop_after in ("s4b", "s4b_scores", "s4b_t1", "s4b_mask"):
                continue
            ev_wbc_used = None
            for e in range(E):
                par = e % 2
                h_t = hA_t if par == 0 else hB_t
                ev_bcw = None
                for n in range(nn):
                    te.wait(ev_w8)
                    te.wait(pfree["PD"])
                    te.wait(ev_wbc_used)
                    ev_t = E_te(lambda q, e=e, n=n, nsz=nsz:
                                q.matmul(PD[:, 0:nsz], OH[:, e * P:(e + 1) * P],
                                         W8[:, n * nsz:(n + 1) * nsz], start=True, stop=True),
                                mark=True)
                    dv.wait(ev_t)
                    ev_bcw = E_dv(lambda q, n=n, nsz=nsz:
                                  q.tensor_copy(WBC[:, n * nsz:(n + 1) * nsz], PD[:, 0:nsz]),
                                  mark=True)
                    pfree["PD"] = ev_bcw
                if stop_after in ("s5_wbc", "s5_wbc_nomm", "s5_wbc_noev", "s5_wbc_pb"):
                    break
                ev_h = [None] * FT
                ev_t3 = None
                for mf in range(FT):
                    blk1, s1 = wload(w1_h[e, mf])
                    te.wait(wslot_ld[s1])
                    te.wait(pfree["PA"])
                    last1 = mm_block(blk1, lambda k: xb_t(k), PA, Cc)
                    wslot_free[s1] = last1
                    blk3, s3 = wload(w3_h[e, mf])
                    te.wait(wslot_ld[s3])
                    te.wait(pfree["PF"])
                    last3 = mm_block(blk3, lambda k: xb_t(k), PF, Cc)
                    wslot_free[s3] = last3
                    ac.wait(last1)
                    ac.wait(ev_hfree[par])
                    if par == 0:
                        ac.wait(ev_sg_free)
                        ac.wait(ev_rowS)
                    ev_s = E_ac(lambda q, mf=mf, h_t=h_t, cc=Cc:
                                q.activation(h_t(mf)[:, 0:cc], PA[:, 0:cc], AF.Silu),
                                mark=True)
                    pfree["PA"] = ev_s
                    dv.wait(last3)
                    dv.wait(ev_bcw)
                    ev_t3 = E_dv(lambda q, cc=Cc:
                                 q.tensor_mul(eph_t(5)[:, 0:cc], PF[:, 0:cc], WBC[:, 0:cc]),
                                 mark=True)
                    pfree["PF"] = ev_t3
                    dv.wait(ev_s)
                    ev_h[mf] = E_dv(lambda q, mf=mf, h_t=h_t, cc=Cc:
                                    q.tensor_mul(h_t(mf)[:, 0:cc], h_t(mf)[:, 0:cc], eph_t(5)[:, 0:cc]),
                                    mark=True)
                ev_wbc_used = ev_t3
                if stop_after == "s5_h":
                    break
                last2 = None
                for m in range(KT):
                    blk2, s2 = wload(w2_h[e, m])
                    te.wait(wslot_ld[s2])
                    te.wait(pfree["PB"])
                    for k in range(KT):
                        te.wait(ev_h[k])
                    last2 = mm_block(blk2, lambda k, h_t=h_t: h_t(k), PB, Cc)
                    wslot_free[s2] = last2
                    dv.wait(last2)
                    dv.wait(ev_x1[m] if e == 0 else ev_y2[m])
                    ev_y2[m] = E_dv(lambda q, m=m, cc=Cc:
                                    q.tensor_add(y_t(m)[:, 0:cc], PB[:, 0:cc], y_t(m)[:, 0:cc]),
                                    mark=True)
                    pfree["PB"] = ev_y2[m]
                ev_hfree[par] = last2
                if stop_after == "s5_e0":
                    break

            if dbg and last_real:
                ev_d = dump(dbg_y2, GV[:], [e_ for e_ in ev_y2])
                ac.wait(ev_d)
            if stop_after == "s5":
                continue
            if stop_after in ("s5", "s5_wbc", "s5_h", "s5_e0",
                              "s5_wbc_nomm", "s5_wbc_noev", "s5_wbc_pb"):
                continue
            ev_rowS2 = None
            for m in range(KT):
                ac.wait(ev_y2[m])
                ac.wait(ev_hfree[0])
                ev_yb2 = E_ac(lambda q, m=m, cc=Cc:
                              q.copy(sg_t(m)[:, 0:cc], y_t(m)[:, 0:cc]),
                              mark=True)
                sl = 3 + (m % 2)
                ac.wait(ev_eph34[m % 2])
                ev_sq2 = E_ac(lambda q, m=m, sl=sl, cc=Cc:
                              q.activation(eph_t(sl)[:, 0:cc], y_t(m)[:, 0:cc], AF.Square),
                              mark=True)
                te.wait(ev_yb2)
                te.wait(ev_sq2)
                if m == 0:
                    te.wait(pfree["PC"])
                    te.wait(pfree["PD"])
                evk = None
                for n in range(nn):
                    rS = 0 if n == 0 else 32
                    psQ = PC[64:65, 0:nsz] if n == 0 else PD[0:1, 0:nsz]
                    E_te(lambda q, m=m, n=n, rS=rS, nsz=nsz, st=(m == 0), sp=(m == KT - 1):
                         q.matmul(PC[rS:rS + 1, 0:nsz], ONB[:, :],
                                  sg_t(m)[:, n * nsz:(n + 1) * nsz], start=st, stop=sp))
                    evk = E_te(lambda q, sl=sl, n=n, psQ=psQ, nsz=nsz, st=(m == 0), sp=(m == KT - 1):
                               q.matmul(psQ, ONB[:, :],
                                        eph_t(sl)[:, n * nsz:(n + 1) * nsz], start=st, stop=sp),
                               mark=(n == nn - 1))
                ev_eph34[m % 2] = evk
                if m == KT - 1:
                    ev_rowS2 = evk
            ev_sg_free_new = ev_rowS2
            ev_bc2, ev_pc2 = ln_rows(ev_rowS2)
            pfree["PC"] = ev_pc2
            ev_out_dma_hi = None
            for m in range(KT):
                dv.wait(ev_bc2)
                E_dv(lambda q, m=m, cc=Cc:
                     q.tensor_mul(out_t(m)[:, 0:cc], y_t(m)[:, 0:cc], BC[:, 0:cc]))
                ev_o = E_dv(lambda q, m=m, cc=Cc:
                            q.tensor_add(out_t(m)[:, 0:cc], out_t(m)[:, 0:cc], BC[:, C:C + cc]),
                            mark=True)
                gp.wait(ev_o)
                ev_dma = E_gp(lambda q, m=m, t0=tok0, cc=Cc:
                              q.dma_start(out=out_h[m * P:(m + 1) * P, t0 - W:t0 - W + cc],
                                          in_=out_t(m)[:, 0:cc]),
                              "out")
                if m >= 8:
                    ev_out_dma_hi = ev_dma
                if m == KT - 1:
                    ev_gv_free = ev_o
            ev_t2_dma = ev_out_dma_hi
            ev_sg_free = ev_sg_free_new

        for w_ in ("wt0", "wt1", "xb", "out", "x", "cn"):
            gp.wait((sems[w_], cnt[w_]))
        gp.wait((s_dv, cnt["dv"]))
        gp.wait((s_ac, cnt["ac"]))
        gp.wait((s_pe, cnt["pe"]))

        block.sync(lambda q: sy.replay(q))
        block.gpsimd(lambda q: gp.replay(q))
        block.tensor(lambda q: te.replay(q))
        block.vector(lambda q: dv.replay(q))
        block.scalar(lambda q: ac.replay(q))
    return nc


def _prep_shared(Wi, Wf, Wg, Wo, gnorm, gate_w, w1, w3, w2):
    f32 = lambda a: np.ascontiguousarray(np.asarray(a, dtype=np.float32))
    sh = {}
    sh["wi_blk"] = _wblocks(f32(Wi))
    sh["wf_blk"] = _wblocks(f32(Wf))
    sh["wg_blk"] = _wblocks(f32(Wg))
    sh["wo_blk"] = _wblocks(f32(Wo))
    w1, w3, w2 = f32(w1), f32(w3), f32(w2)
    sh["w1_blk"] = np.ascontiguousarray(np.stack([_wblocks(w1[e]) for e in range(E)]))
    sh["w3_blk"] = np.ascontiguousarray(np.stack([_wblocks(w3[e]) for e in range(E)]))
    sh["w2_blk"] = np.ascontiguousarray(np.stack([_wblocks(w2[e]) for e in range(E)]))
    sh["gw_t"] = np.ascontiguousarray(f32(gate_w).T.reshape(KT, P, E))
    sh["gn_t"] = np.ascontiguousarray(f32(gnorm).reshape(KT, P).T)
    sh["idt"] = np.eye(P, dtype=np.float32)
    sh["idtb"] = np.eye(P, dtype=np.float32)
    oh = np.zeros((E, E * P), np.float32)
    for e in range(E):
        oh[e, e * P:(e + 1) * P] = 1.0
    sh["oh"] = oh
    sh["on1"] = np.ones((1, P), np.float32)
    sh["epsc"] = np.full((1, 1), 1e-5, np.float32)
    sh["onb"] = np.ones((P, 1), np.float32)
    return sh


def run_module(x, Wi, Wf, Wg, Wo, gnorm, gate_w, w1, w3, w2, W=512, C=1024, dbg=False, stop_after=None):
    x = np.asarray(x, dtype=np.float32)
    B, T, _ = x.shape
    per_b = NCORES // B
    T_core = T // per_b
    sh = _prep_shared(Wi, Wf, Wg, Wo, gnorm, gate_w, w1, w3, w2)
    ims = []
    for cid in range(NCORES):
        b, q = cid // per_b, cid % per_b
        t0 = q * T_core
        warm = np.zeros((W, D), np.float32) if t0 == 0 else x[b, t0 - W:t0]
        xc = np.concatenate([warm, x[b, t0:t0 + T_core]], axis=0)
        im = dict(sh)
        im["x_fm"] = np.ascontiguousarray(xc.T)
        ims.append(im)
    nc = build_nc(T_core, W, C, dbg=dbg, stop_after=stop_after)
    res = run_bass_kernel_spmd(nc, ims, core_ids=list(range(NCORES)))
    out = np.empty((B, T, D), np.float32)
    for cid in range(NCORES):
        b, q = cid // per_b, cid % per_b
        out[b, q * T_core:(q + 1) * T_core] = np.asarray(res.results[cid]["out_fm"]).T
    if dbg:
        return out, res.results
    return out


def kernel(L0_attn_hidden_states, L0_attn_Wi, L0_attn_Wf, L0_attn_Wg, L0_attn_Wo,
           L0_attn_gnorm_w, L0_ffn_gate_w, L0_ffn_w1, L0_ffn_w3, L0_ffn_w2):
    return run_module(L0_attn_hidden_states, L0_attn_Wi, L0_attn_Wf, L0_attn_Wg,
                      L0_attn_Wo, L0_attn_gnorm_w, L0_ffn_gate_w, L0_ffn_w1,
                      L0_ffn_w3, L0_ffn_w2, W=512, C=1024)


# version 03
# speedup vs baseline: 1.0682x; 1.0682x over previous
from contextlib import ExitStack
import numpy as np
import ml_dtypes
import concourse.bass as bass
import concourse.mybir as mybir
from concourse.bass_utils import run_bass_kernel_spmd

F32 = mybir.dt.float32
BF16 = mybir.dt.bfloat16
AF = mybir.ActivationFunctionType
ALU = mybir.AluOpType
AX = mybir.AxisListType.X

P = 128
D = 2048
KT = D // P
E = 8
FT = 2048 // P
EPS = 1e-5
NEG_BIG = -1e30
NCORES = 8


def _wblocks(Wmat):
    M = Wmat.shape[0] // P
    K = Wmat.shape[1] // P
    t = Wmat.reshape(M, P, K, P)
    return np.ascontiguousarray(t.transpose(0, 3, 2, 1)).reshape(M, P, K * P)


class _Eng:
    def __init__(self):
        self.thunks = []
        self.waited = {}

    def wait(self, ev):
        if ev is None:
            return
        sem, v = ev
        if v <= 0 or self.waited.get(id(sem), 0) >= v:
            return
        self.waited[id(sem)] = v
        self.thunks.append(("w", (sem, v)))

    def replay(self, q):
        for kind, x in self.thunks:
            if kind == "w":
                q.wait_ge(x[0], x[1])
            else:
                x(q)


def build_nc(T_core, W, C, dbg=False, stop_after=None):
    assert T_core % C == 0 and W <= C
    NSZ = min(512, C)
    NS = C // NSZ
    JT = C // P
    T_all = W + T_core
    NWB = 2

    nc = bass.Bass("TRN2", target_bir_lowering=False)
    ap = lambda t: t.ap()

    x_h = ap(nc.dram_tensor("x_fm", [D, T_all], F32, kind="ExternalInput"))
    wi_h = ap(nc.dram_tensor("wi_blk", [KT, P, D], F32, kind="ExternalInput"))
    wf_h = ap(nc.dram_tensor("wf_blk", [KT, P, D], F32, kind="ExternalInput"))
    wg_h = ap(nc.dram_tensor("wg_blk", [KT, P, D], F32, kind="ExternalInput"))
    wo_h = ap(nc.dram_tensor("wo_blk", [KT, P, D], F32, kind="ExternalInput"))
    w1_h = ap(nc.dram_tensor("w1_blk", [E, FT, P, D], F32, kind="ExternalInput"))
    w3_h = ap(nc.dram_tensor("w3_blk", [E, FT, P, D], F32, kind="ExternalInput"))
    w2_h = ap(nc.dram_tensor("w2_blk", [E, KT, P, 2048], F32, kind="ExternalInput"))
    gw_h = ap(nc.dram_tensor("gw_t", [KT, P, E], F32, kind="ExternalInput"))
    gn_h = ap(nc.dram_tensor("gn_t", [P, KT], F32, kind="ExternalInput"))
    idt_h = ap(nc.dram_tensor("idt", [P, P], F32, kind="ExternalInput"))
    idtb_h = ap(nc.dram_tensor("idtb", [P, P], F32, kind="ExternalInput"))
    oh_h = ap(nc.dram_tensor("oh", [E, E * P], F32, kind="ExternalInput"))
    on1_h = ap(nc.dram_tensor("on1", [1, P], F32, kind="ExternalInput"))
    onb_h = ap(nc.dram_tensor("onb", [P, 1], F32, kind="ExternalInput"))
    eps_h = ap(nc.dram_tensor("epsc", [1, 1], F32, kind="ExternalInput"))
    out_h = ap(nc.dram_tensor("out_fm", [D, T_core], F32, kind="ExternalOutput"))
    if dbg:
        dbg_o = ap(nc.dram_tensor("dbg_o", [P, KT * C], F32, kind="ExternalOutput"))
        dbg_v = ap(nc.dram_tensor("dbg_v", [P, KT * C // 2], F32, kind="ExternalOutput"))
        dbg_gate = ap(nc.dram_tensor("dbg_gate", [P, 8 * C], F32, kind="ExternalOutput"))
        dbg_o2 = ap(nc.dram_tensor("dbg_o2", [P, KT * C], F32, kind="ExternalOutput"))
        dbg_y = ap(nc.dram_tensor("dbg_y", [P, KT * C], F32, kind="ExternalOutput"))
        dbg_x1 = ap(nc.dram_tensor("dbg_x1", [P, KT * C], F32, kind="ExternalOutput"))
        dbg_w8 = ap(nc.dram_tensor("dbg_w8", [E, C], F32, kind="ExternalOutput"))
        dbg_y2 = ap(nc.dram_tensor("dbg_y2", [P, KT * C], F32, kind="ExternalOutput"))
        dbg_st = ap(nc.dram_tensor("dbg_st", [P, KT], F32, kind="ExternalOutput"))
        dbg_lt = ap(nc.dram_tensor("dbg_lt", [P, (C // P) * E], F32, kind="ExternalOutput"))
        dbg_et = ap(nc.dram_tensor("dbg_et", [P, (C // P) * E], F32, kind="ExternalOutput"))
        dbg_den = ap(nc.dram_tensor("dbg_den", [P, C // P], F32, kind="ExternalOutput"))
        dbg_msk = ap(nc.dram_tensor("dbg_msk", [P, (C // P) * E], F32, kind="ExternalOutput"))

    with ExitStack() as _stack:
        en = _stack.enter_context
        GV = en(nc.sbuf_tensor("GV", [P, KT * C], F32))
        T2 = en(nc.sbuf_tensor("T2", [P, KT * C // 2], F32))
        XB = en(nc.sbuf_tensor("XB", [P, KT * C], BF16))
        SG = en(nc.sbuf_tensor("SG", [P, KT * C], BF16))
        EPH = en(nc.sbuf_tensor("EPH", [P, 6 * C], BF16))
        WBUF = en(nc.sbuf_tensor("WBUF", [P, NWB * D], BF16))
        BC = en(nc.sbuf_tensor("BC", [P, 2 * C], F32))
        AB = en(nc.sbuf_tensor("AB", [1, 2 * NSZ], F32))
        MRow = en(nc.sbuf_tensor("MRow", [1, NSZ], BF16))
        WBC = en(nc.sbuf_tensor("WBC", [P, C], BF16))
        L8 = en(nc.sbuf_tensor("L8", [E, C], BF16))
        W8 = en(nc.sbuf_tensor("W8", [E, C], BF16))
        LT = en(nc.sbuf_tensor("LT", [P, JT * E], BF16))
        ET = en(nc.sbuf_tensor("ET", [P, JT * E], F32))
        MSK = en(nc.sbuf_tensor("MSK", [P, JT * E], F32))
        M1 = en(nc.sbuf_tensor("M1", [P, JT], F32))
        M1N = en(nc.sbuf_tensor("M1N", [P, JT], F32))
        M2 = en(nc.sbuf_tensor("M2", [P, JT], F32))
        DEN = en(nc.sbuf_tensor("DEN", [P, JT], F32))
        STATE = en(nc.sbuf_tensor("STATE", [P, KT], F32))
        GN = en(nc.sbuf_tensor("GN", [P, KT], F32))
        IDTB = en(nc.sbuf_tensor("IDTB", [P, P], BF16))
        OH = en(nc.sbuf_tensor("OH", [E, E * P], BF16))
        ON1 = en(nc.sbuf_tensor("ON1", [1, P], F32))
        ONB = en(nc.sbuf_tensor("ONB", [P, 1], BF16))
        GWT = en(nc.sbuf_tensor("GWT", [P, KT * E], BF16))
        EPSC = en(nc.sbuf_tensor("EPSC", [1, 1], F32))
        PA = en(nc.psum_tensor("PA", [P, 1024], F32))
        PF = en(nc.psum_tensor("PF", [P, 1024], F32))
        PB = en(nc.psum_tensor("PB", [P, 1024], F32))
        PC = en(nc.psum_tensor("PC", [P, 512], F32))
        PD = en(nc.psum_tensor("PD", [P, 512], F32))
        s_pe = en(nc.semaphore("s_pe"))
        s_dv = en(nc.semaphore("s_dv"))
        s_ac = en(nc.semaphore("s_ac"))
        s_wt0 = en(nc.semaphore("s_wt0"))
        s_wt1 = en(nc.semaphore("s_wt1"))
        s_xb = en(nc.semaphore("s_xb"))
        s_out = en(nc.semaphore("s_out"))
        s_x = en(nc.semaphore("s_x"))
        s_cn = en(nc.semaphore("s_cn"))
        block = en(nc.Block())
        GVb = GV.bitcast(BF16)
        T2b = T2.bitcast(BF16)
        XBf = XB.bitcast(F32)
        PDb = PD.bitcast(BF16)

        te, dv, ac, gp, sy = _Eng(), _Eng(), _Eng(), _Eng(), _Eng()
        cnt = {"pe": 0, "dv": 0, "ac": 0, "wt0": 0, "wt1": 0, "xb": 0, "out": 0, "x": 0, "cn": 0}
        sems = {"pe": s_pe, "dv": s_dv, "ac": s_ac, "wt0": s_wt0, "wt1": s_wt1,
                "xb": s_xb, "out": s_out, "x": s_x, "cn": s_cn}
        engs = {"pe": te, "dv": dv, "ac": ac}
        step = {"pe": 1, "dv": 1, "ac": 1, "wt0": 16, "wt1": 16, "xb": 16, "out": 16, "x": 16, "cn": 16}

        def emit(which, fn, mark=False, eng=None):
            eng = eng if eng is not None else engs[which]
            if which not in ("dv", "ac") and not mark:
                eng.thunks.append(("i", fn))
                return None
            sem = sems[which]
            st = step[which]
            def g(q, fn=fn, sem=sem, st=st):
                fn(q).then_inc(sem, st)
            eng.thunks.append(("i", g))
            cnt[which] += st
            ev = (sem, cnt[which])
            if which in ("dv", "ac"):
                eng.thunks.append(("w", ev))
                eng.waited[id(sem)] = cnt[which]
            return ev

        E_te = lambda fn, mark=False: emit("pe", fn, mark)

        def dvs(fn):
            ev = emit("dv", fn, True)
            dv.wait(ev)
            return ev
        E_dv = lambda fn, mark=False: emit("dv", fn, mark)
        E_ac = lambda fn, mark=False: emit("ac", fn, mark)
        E_gp = lambda fn, cls: emit(cls, fn, mark=True, eng=gp)
        E_sy = lambda fn, cls: emit(cls, fn, mark=True, eng=sy)

        def gate_t(mi):
            return GV[:, (8 + mi) * C:(9 + mi) * C]

        def o_t(k):
            return GVb[:, k * C:(k + 1) * C]

        def y_t(m):
            return GV[:, m * C:(m + 1) * C]

        def v_t(mi):
            return T2b[:, mi * C:(mi + 1) * C]

        def o2_t(k):
            return T2b[:, k * C:(k + 1) * C]

        def hB_t(k):
            return T2b[:, k * C:(k + 1) * C]

        def hA_t(k):
            return SG[:, k * C:(k + 1) * C]

        def sg_t(k):
            return SG[:, k * C:(k + 1) * C]

        def xb_t(k):
            return XB[:, k * C:(k + 1) * C]

        def eph_t(i):
            return EPH[:, i * C:(i + 1) * C]

        def out_t(m):
            return XBf[:, m * C:(m + 1) * C] if m < 8 else T2[:, (m - 8) * C:(m - 7) * C]

        wslot_free = [None] * NWB
        wslot_ld = [None] * NWB
        wb_i = [0]

        def dump(dst_ap, src_ap, evs):
            for e_ in evs:
                gp.wait(e_)
            return E_gp(lambda q, d=dst_ap, s=src_ap: q.dma_start(out=d, in_=s), "xb")

        def wload(src):
            s = wb_i[0] % NWB
            gp.wait(wslot_free[s])
            dst = WBUF[:, s * D:(s + 1) * D]
            ev = E_gp(lambda q, dst=dst, src=src: q.dma_start(out=dst, in_=src),
                      "wt0" if s == 0 else "wt1")
            wslot_ld[s] = ev
            wb_i[0] += 1
            return dst, s

        def mm_block(blk, rhs_t, ps, Cc, mark_each=False):
            nsz = min(512, Cc)
            nn = Cc // nsz
            last = None
            for k in range(KT):
                for n in range(nn):
                    mark = (k == KT - 1 and n == nn - 1)
                    ev = E_te(lambda q, blk=blk, ps=ps, k=k, n=n, nsz=nsz,
                              st=(k == 0), sp=(k == KT - 1), rhs_t=rhs_t:
                              q.matmul(ps[:, n * nsz:(n + 1) * nsz],
                                       blk[:, k * P:(k + 1) * P],
                                       rhs_t(k)[:, n * nsz:(n + 1) * nsz],
                                       start=st, stop=sp),
                              mark=mark)
                    if mark:
                        last = ev
            return last

        pfree = {"PA": None, "PF": None, "PB": None, "PC": None, "PD": None}
        ev_gate_free = [None] * 8
        ev_gv_free = None
        ev_sg_free = None
        ev_t2_dma = None
        ev_hfree = [None, None]
        ev_eph34 = [None, None]
        ev_eph012 = [None, None, None]
        ev_state0 = None

        ev_const = None
        for dst, src in ((GN[:], gn_h), (ON1[:], on1_h),
                         (OH[:], oh_h), (ONB[:], onb_h), (IDTB[:], idtb_h),
                         (GWT[:].rearrange("p (k e) -> p k e", k=KT),
                          gw_h.rearrange("k p e -> p k e")),
                         (EPSC[:], eps_h)):
            ev_const = E_gp(lambda q, d=dst, s=src: q.dma_start(out=d, in_=s), "cn")
        ev_state0 = E_dv(lambda q: q.memset(STATE[:], 0.0), mark=True)

        x_kpt = x_h.rearrange("(k p) t -> p k t", p=P)

        nchunks = T_core // C
        chunks = [("warm", 0, W)] + [("real", W + i * C, C) for i in range(nchunks)]
        ev_y2 = [None] * KT

        for ci, (kind, tok0, Cc) in enumerate(chunks):
            warm = kind == "warm"
            nsz = min(512, Cc)
            nn = Cc // nsz
            jt = Cc // P

            gp.wait(ev_const)
            ev_xb = E_gp(lambda q, t0=tok0, cc=Cc:
                         q.dma_start(out=XB[:].rearrange("p (k c) -> p k c", k=KT)[:, :, 0:cc],
                                     in_=x_kpt[:, :, t0:t0 + cc]),
                         "xb")

            mats = [("i", wi_h), ("f", wf_h)] if warm else [("i", wi_h), ("f", wf_h), ("g", wg_h)]
            ev_scan = [None] * KT
            ev_sg = [None] * KT
            ev_o2 = [None] * KT
            for half in range(2):
                for mi in range(8):
                    m = half * 8 + mi
                    ev_mm = {}
                    for mat, hsrc in mats:
                        blk, slot = wload(hsrc[m])
                        ps = {"i": PA, "f": PF, "g": PB}[mat]
                        te.wait(wslot_ld[slot])
                        te.wait(ev_xb)
                        te.wait(pfree["PA" if mat == "i" else "PF" if mat == "f" else "PB"])
                        last = mm_block(blk, lambda k: xb_t(k), ps, Cc)
                        wslot_free[slot] = last
                        ev_mm[mat] = last
                    ac.wait(ev_mm["f"])
                    ac.wait(ev_gate_free[mi])
                    ac.wait(ev_gv_free)
                    ac.wait(ev_eph012[0])
                    ac.wait(ev_eph012[1])
                    ev_g1 = E_ac(lambda q, mi=mi, cc=Cc:
                                 q.activation(gate_t(mi)[:, 0:cc], PF[:, 0:cc], AF.Sigmoid),
                                 mark=True)
                    ev_e0 = E_ac(lambda q, cc=Cc:
                                 q.activation(eph_t(0)[:, 0:cc], PF[:, 0:cc], AF.Sigmoid, scale=-1.0),
                                 mark=True)
                    pfree["PF"] = ev_e0
                    ac.wait(ev_mm["i"])
                    ev_e1 = E_ac(lambda q, cc=Cc:
                                 q.activation(eph_t(1)[:, 0:cc], PA[:, 0:cc], AF.Silu),
                                 mark=True)
                    pfree["PA"] = ev_e1
                    dv.wait(ev_e0)
                    dv.wait(ev_e1)
                    dv.wait(ev_t2_dma)
                    ev_v = E_dv(lambda q, mi=mi, cc=Cc:
                                q.tensor_mul(v_t(mi)[:, 0:cc], eph_t(1)[:, 0:cc], eph_t(0)[:, 0:cc]),
                                mark=True)
                    ev_eph012[0] = ev_v
                    ev_eph012[1] = ev_v
                    if not warm:
                        ac.wait(ev_mm["g"])
                        ac.wait(ev_eph012[2])
                        ev_e2 = E_ac(lambda q, cc=Cc:
                                     q.activation(eph_t(2)[:, 0:cc], PB[:, 0:cc], AF.Silu),
                                     mark=True)
                        pfree["PB"] = ev_e2
                        dv.wait(ev_e2)
                        dv.wait(ev_sg_free)
                        dv.wait(ev_hfree[0])
                        ev_sg[m] = E_dv(lambda q, m=m, cc=Cc:
                                        q.tensor_single_scalar(sg_t(m)[:, 0:cc], eph_t(2)[:, 0:cc],
                                                               GN[:, m:m + 1], ALU.mult),
                                        mark=True)
                        ev_eph012[2] = ev_sg[m]
                    dv.wait(ev_g1)
                    dv.wait(ev_state0)
                    dvs(lambda q, m=m, mi=mi, cc=Cc:
                        q.tensor_tensor_scan(o_t(m)[:, 0:cc], gate_t(mi)[:, 0:cc],
                                             v_t(mi)[:, 0:cc], STATE[:, m:m + 1],
                                             ALU.mult, ALU.add))
                    ev_scan[m] = dvs(lambda q, m=m, cc=Cc:
                                     q.tensor_copy(STATE[:, m:m + 1], o_t(m)[:, cc - 1:cc]))
                    ev_gate_free[mi] = ev_scan[m]
            if warm:
                if dbg:
                    ev_d = dump(dbg_st, STATE[:], [ev_scan[KT - 1]])
                    dv.wait(ev_d)
                    if dbg == 2:
                        ev_d = dump(dbg_v, T2b[:, 0:KT * C // 2], [ev_scan[KT - 1]])
                        ev_d2 = dump(dbg_o, GVb[:, 0:KT * C], [ev_scan[KT - 1]])
                        ev_d3 = dump(dbg_gate, GV[:, 8 * C:KT * C], [ev_scan[KT - 1]])
                        dv.wait(ev_d)
                        dv.wait(ev_d2)
                        ac.wait(ev_d3)
                        sy.wait(ev_d3)
                continue
            last_real = (ci == len(chunks) - 1)
            if dbg == 2:
                last_real = False
            if dbg and last_real:
                ev_d = dump(dbg_v, T2b[:, 0:KT * C // 2], [ev_scan[KT - 1]])
                ev_d2 = dump(dbg_o, GVb[:, 0:KT * C], [ev_scan[KT - 1]])
                ev_d3 = dump(dbg_gate, GV[:, 8 * C:KT * C], [ev_scan[KT - 1]])
                dv.wait(ev_d)
                dv.wait(ev_d2)
                ac.wait(ev_d3)
                sy.wait(ev_d3)

            if stop_after == "s1":
                continue
            ev_row = None
            for k in range(KT):
                sl = 3 + (k % 2)
                ac.wait(ev_scan[k])
                ac.wait(ev_eph34[k % 2])
                ev_sq = E_ac(lambda q, k=k, sl=sl, cc=Cc:
                             q.activation(eph_t(sl)[:, 0:cc], o_t(k)[:, 0:cc], AF.Square),
                             mark=True)
                te.wait(ev_sq)
                if k == 0:
                    te.wait(pfree["PC"])
                evk = None
                for n in range(nn):
                    r = 0 if n == 0 else 32
                    evk = E_te(lambda q, sl=sl, n=n, r=r, nsz=nsz, st=(k == 0), sp=(k == KT - 1):
                               q.matmul(PC[r:r + 1, 0:nsz], ONB[:, :],
                                        eph_t(sl)[:, n * nsz:(n + 1) * nsz],
                                        start=st, stop=sp),
                               mark=(n == nn - 1))
                ev_eph34[k % 2] = evk
                if k == KT - 1:
                    ev_row = evk
            ev_bcr = None
            ev_pcr = None
            for n in range(nn):
                r = 0 if n == 0 else 32
                dv.wait(ev_row)
                ev_pcr = E_dv(lambda q, r=r, nsz=nsz:
                              q.tensor_copy(AB[0:1, 0:nsz], PC[r:r + 1, 0:nsz]),
                              mark=(n == nn - 1))
                ev_r = E_dv(lambda q, nsz=nsz: q.tensor_scalar_mul(AB[0:1, 0:nsz], AB[0:1, 0:nsz], 1.0 / D),
                            mark=True)
                ac.wait(ev_r)
                E_ac(lambda q, nsz=nsz: q.activation(AB[0:1, 0:nsz], AB[0:1, 0:nsz], AF.Ln, bias=EPSC[0:1, 0:1]))
                ev_rr = E_ac(lambda q, nsz=nsz:
                             q.activation(AB[0:1, 0:nsz], AB[0:1, 0:nsz], AF.Exp, scale=-0.5),
                             mark=True)
                te.wait(ev_rr)
                te.wait(pfree["PD"])
                ev_t = E_te(lambda q, nsz=nsz:
                            q.matmul(PD[:, 0:nsz], ON1[:, :], AB[0:1, 0:nsz],
                                     start=True, stop=True),
                            mark=True)
                dv.wait(ev_t)
                ev_bcr = E_dv(lambda q, n=n, nsz=nsz:
                              q.tensor_copy(BC[:, n * nsz:(n + 1) * nsz], PD[:, 0:nsz]),
                              mark=True)
                pfree["PD"] = ev_bcr
                ac.wait(ev_t)
                dv.wait(ev_t)
            pfree["PC"] = ev_pcr
            for k in range(KT):
                dv.wait(ev_bcr)
                dv.wait(ev_sg[k])
                E_dv(lambda q, k=k, cc=Cc:
                     q.tensor_mul(eph_t(5)[:, 0:cc], o_t(k)[:, 0:cc], BC[:, 0:cc]))
                ev_o2[k] = E_dv(lambda q, k=k, cc=Cc:
                                q.tensor_mul(o2_t(k)[:, 0:cc], eph_t(5)[:, 0:cc], sg_t(k)[:, 0:cc]),
                                mark=True)

            if dbg and last_real:
                ev_d = dump(dbg_o2, T2b[:, 0:KT * C], [e_ for e_ in ev_o2])
                dv.wait(ev_d)
                te.wait(ev_d)
            if stop_after == "s3":
                continue
            ev_y = [None] * KT
            ev_x1 = [None] * KT
            ev_x1b = [None] * KT
            ev_rowS = None
            for k in range(KT):
                sy.wait(ev_o2[k])
            sy.wait(ev_scan[KT - 1])
            ev_x = E_sy(lambda q, t0=tok0, cc=Cc:
                        q.dma_start(out=GV[:].rearrange("p (k c) -> p k c", k=KT)[:, :, 0:cc],
                                    in_=x_kpt[:, :, t0:t0 + cc]),
                        "x")
            for m in range(KT):
                pname = "PA" if m % 2 == 0 else "PF"
                ps = PA if m % 2 == 0 else PF
                blk, slot = wload(wo_h[m])
                te.wait(wslot_ld[slot])
                te.wait(pfree[pname])
                for k in range(KT):
                    te.wait(ev_o2[k])
                last = mm_block(blk, lambda k: o2_t(k), ps, Cc)
                wslot_free[slot] = last
                dv.wait(last)
                dv.wait(ev_x)
                ev_y[m] = E_dv(lambda q, m=m, ps=ps, cc=Cc:
                               q.tensor_add(y_t(m)[:, 0:cc], ps[:, 0:cc], y_t(m)[:, 0:cc]),
                               mark=True)
                pfree[pname] = ev_y[m]
                ac.wait(ev_y[m])
                ac.wait(ev_o2[m])
                ev_yb = E_ac(lambda q, m=m, cc=Cc:
                             q.copy(sg_t(m)[:, 0:cc], y_t(m)[:, 0:cc]),
                             mark=True)
                sl = 3 + (m % 2)
                ac.wait(ev_eph34[m % 2])
                ev_sqy = E_ac(lambda q, m=m, sl=sl, cc=Cc:
                              q.activation(eph_t(sl)[:, 0:cc], y_t(m)[:, 0:cc], AF.Square),
                              mark=True)
                te.wait(ev_yb)
                te.wait(ev_sqy)
                if m == 0:
                    te.wait(pfree["PC"])
                    te.wait(pfree["PD"])
                evk = None
                for n in range(nn):
                    rS = 0 if n == 0 else 32
                    psQ = PC[64:65, 0:nsz] if n == 0 else PD[0:1, 0:nsz]
                    E_te(lambda q, m=m, n=n, rS=rS, nsz=nsz, st=(m == 0), sp=(m == KT - 1):
                         q.matmul(PC[rS:rS + 1, 0:nsz], ONB[:, :],
                                  sg_t(m)[:, n * nsz:(n + 1) * nsz], start=st, stop=sp))
                    evk = E_te(lambda q, sl=sl, n=n, psQ=psQ, nsz=nsz, st=(m == 0), sp=(m == KT - 1):
                               q.matmul(psQ, ONB[:, :],
                                        eph_t(sl)[:, n * nsz:(n + 1) * nsz], start=st, stop=sp),
                               mark=(n == nn - 1))
                ev_eph34[m % 2] = evk
                if m == KT - 1:
                    ev_rowS = evk

            def ln_rows(ev_rows):
                ev_bc = None
                ev_pc_read = None
                for n in reversed(range(nn)):
                    rS = 0 if n == 0 else 32
                    psQ = PC[64:65, 0:nsz] if n == 0 else PD[0:1, 0:nsz]
                    dv.wait(ev_rows)
                    E_dv(lambda q, psQ=psQ, nsz=nsz:
                         q.tensor_copy(AB[0:1, 0:nsz], psQ))
                    ev_pc_read = E_dv(lambda q, rS=rS, nsz=nsz:
                                      q.tensor_copy(AB[0:1, nsz:2 * nsz],
                                                    PC[rS:rS + 1, 0:nsz]),
                                      mark=True)
                    if n == nn - 1 and nn > 1:
                        pfree["PD"] = ev_pc_read
                    E_dv(lambda q, nsz=nsz:
                         q.tensor_scalar_mul(AB[0:1, 0:nsz], AB[0:1, 0:nsz], 1.0 / D))
                    E_dv(lambda q, nsz=nsz:
                         q.tensor_scalar_mul(AB[0:1, nsz:2 * nsz], AB[0:1, nsz:2 * nsz], 1.0 / D))
                    E_dv(lambda q, nsz=nsz:
                         q.tensor_mul(MRow[0:1, 0:nsz],
                                      AB[0:1, nsz:2 * nsz], AB[0:1, nsz:2 * nsz]))
                    ev_var = E_dv(lambda q, nsz=nsz:
                                  q.tensor_sub(AB[0:1, 0:nsz], AB[0:1, 0:nsz],
                                               MRow[0:1, 0:nsz]),
                                  mark=True)
                    ac.wait(ev_var)
                    E_ac(lambda q, nsz=nsz:
                         q.activation(AB[0:1, 0:nsz], AB[0:1, 0:nsz], AF.Ln, bias=EPSC[0:1, 0:1]))
                    ev_A = E_ac(lambda q, nsz=nsz:
                                q.activation(AB[0:1, 0:nsz], AB[0:1, 0:nsz], AF.Exp, scale=-0.5),
                                mark=True)
                    dv.wait(ev_A)
                    E_dv(lambda q, nsz=nsz:
                         q.tensor_mul(AB[0:1, nsz:2 * nsz], AB[0:1, nsz:2 * nsz],
                                      AB[0:1, 0:nsz]))
                    ev_B = E_dv(lambda q, nsz=nsz:
                                q.tensor_scalar_mul(AB[0:1, nsz:2 * nsz],
                                                    AB[0:1, nsz:2 * nsz], -1.0),
                                mark=True)
                    for off_ab, off_bc, evr in ((0, 0, ev_A), (nsz, C, ev_B)):
                        te.wait(evr)
                        te.wait(pfree["PD"])
                        ev_t = E_te(lambda q, off_ab=off_ab, nsz=nsz:
                                    q.matmul(PD[:, 0:nsz], ON1[:, :],
                                             AB[0:1, off_ab:off_ab + nsz],
                                             start=True, stop=True),
                                    mark=True)
                        dv.wait(ev_t)
                        ev_bc = E_dv(lambda q, off_bc=off_bc, n=n, nsz=nsz:
                                     q.tensor_copy(BC[:, off_bc + n * nsz:off_bc + (n + 1) * nsz],
                                                   PD[:, 0:nsz]),
                                     mark=True)
                        pfree["PD"] = ev_bc
                    ac.wait(ev_t)
                    dv.wait(ev_t)
                return ev_bc, ev_pc_read

            if dbg and last_real:
                ev_d = dump(dbg_y, GV[:], [e_ for e_ in ev_y])
                dv.wait(ev_d)
            ev_bc1, ev_pc1 = ln_rows(ev_rowS)
            pfree["PC"] = ev_pc1
            for m in range(KT):
                dv.wait(ev_bc1)
                E_dv(lambda q, m=m, cc=Cc:
                     q.tensor_mul(y_t(m)[:, 0:cc], y_t(m)[:, 0:cc], BC[:, 0:cc]))
                ev_x1[m] = E_dv(lambda q, m=m, cc=Cc:
                                q.tensor_add(y_t(m)[:, 0:cc], y_t(m)[:, 0:cc], BC[:, C:C + cc]),
                                mark=True)
                ac.wait(ev_x1[m])
                ev_x1b[m] = E_ac(lambda q, m=m, cc=Cc:
                                 q.copy(xb_t(m)[:, 0:cc], y_t(m)[:, 0:cc]),
                                 mark=True)

            if dbg and last_real:
                ev_d = dump(dbg_x1, GV[:], [e_ for e_ in ev_x1])
                dv.wait(ev_d)
                te.wait(ev_d)
            if stop_after == "s4":
                continue
            ev_l8 = None
            te.wait(pfree["PC"])
            for n in range(nn):
                for m in range(KT):
                    te.wait(ev_x1b[m])
                rb = 0 if n == 0 else 32
                last = None
                for k in range(KT):
                    last = E_te(lambda q, k=k, n=n, rb=rb, nsz=nsz, st=(k == 0), sp=(k == KT - 1):
                                q.matmul(PC[rb:rb + E, 0:nsz], GWT[:, k * E:(k + 1) * E],
                                         xb_t(k)[:, n * nsz:(n + 1) * nsz],
                                         start=st, stop=sp),
                                mark=(k == KT - 1))
                dv.wait(last)
                ev_l8 = E_dv(lambda q, n=n, rb=rb, nsz=nsz:
                             q.tensor_copy(L8[:, n * nsz:(n + 1) * nsz], PC[rb:rb + E, 0:nsz]),
                             mark=True)
            pfree["PC"] = ev_l8
            if stop_after == "s4b_scores":
                continue
            ev_w8 = None
            for j in range(jt):
                te.wait(ev_l8)
                te.wait(pfree["PD"])
                ev_t = E_te(lambda q, j=j:
                            q.transpose(PDb[:, 0:E], L8[:, j * P:(j + 1) * P], IDTB[0:E, 0:E]),
                            mark=True)
                dv.wait(ev_t)
                ev_lt = dvs(lambda q, j=j:
                            q.tensor_copy(LT[:, j * E:(j + 1) * E], PDb[:, 0:E]))
                pfree["PD"] = ev_lt
                if stop_after == "s4b_t1":
                    continue
                dvs(lambda q, j=j: q.tensor_reduce(M1[:, j:j + 1], LT[:, j * E:(j + 1) * E], AX, ALU.max))
                dvs(lambda q, j=j: q.tensor_scalar_mul(M1N[:, j:j + 1], M1[:, j:j + 1], -1.0))
                dvs(lambda q, j=j: q.tensor_single_scalar(MSK[:, j * E:(j + 1) * E],
                                                          LT[:, j * E:(j + 1) * E], M1[:, j:j + 1], ALU.is_ge))
                dvs(lambda q, j=j: q.scalar_tensor_tensor(MSK[:, j * E:(j + 1) * E],
                                                          MSK[:, j * E:(j + 1) * E], NEG_BIG,
                                                          LT[:, j * E:(j + 1) * E], ALU.mult, ALU.add))
                dvs(lambda q, j=j: q.tensor_reduce(M2[:, j:j + 1], MSK[:, j * E:(j + 1) * E], AX, ALU.max))
                ev_mk = dvs(lambda q, j=j: q.tensor_single_scalar(MSK[:, j * E:(j + 1) * E],
                                                                  LT[:, j * E:(j + 1) * E],
                                                                  M2[:, j:j + 1], ALU.is_ge))
                if stop_after == "s4b_mask":
                    continue
                if dbg and last_real and j == jt - 1:
                    ev_d = dump(dbg_lt, LT[:], [ev_mk])
                    ev_d2 = dump(dbg_et, MSK[:], [ev_mk])
                    ac.wait(ev_d)
                    ac.wait(ev_d2)
                    dv.wait(ev_d)
                    dv.wait(ev_d2)
                ac.wait(ev_mk)
                ev_ex = E_ac(lambda q, j=j:
                             q.activation(ET[:, j * E:(j + 1) * E], LT[:, j * E:(j + 1) * E], AF.Exp,
                                          bias=M1N[:, j:j + 1], accum_out=DEN[:, j:j + 1]),
                             mark=True)
                dv.wait(ev_ex)
                dvs(lambda q, j=j: q.reciprocal(DEN[:, j:j + 1], DEN[:, j:j + 1]))
                dvs(lambda q, j=j: q.tensor_mul(MSK[:, j * E:(j + 1) * E],
                                                MSK[:, j * E:(j + 1) * E], ET[:, j * E:(j + 1) * E]))
                dvs(lambda q, j=j: q.tensor_single_scalar(MSK[:, j * E:(j + 1) * E],
                                                          MSK[:, j * E:(j + 1) * E],
                                                          DEN[:, j:j + 1], ALU.mult))
                ev_c = dvs(lambda q, j=j: q.tensor_copy(LT[:, j * E:(j + 1) * E],
                                                        MSK[:, j * E:(j + 1) * E]))
                te.wait(ev_c)
                te.wait(pfree["PD"])
                ev_t2_ = E_te(lambda q, j=j:
                              q.transpose(PDb[0:E, 0:P], LT[:, j * E:(j + 1) * E], IDTB[:, :]),
                              mark=True)
                dv.wait(ev_t2_)
                ev_w8 = E_dv(lambda q, j=j:
                             q.tensor_copy(W8[:, j * P:(j + 1) * P], PDb[0:E, 0:P]),
                             mark=True)
                pfree["PD"] = ev_w8
                if dbg and last_real and j == jt - 1:
                    dump(dbg_msk, MSK[:], [ev_w8])

            if dbg and last_real:
                ev_d = dump(dbg_w8, W8[:], [ev_w8])
                te.wait(ev_d)
                dv.wait(ev_d)
                dump(dbg_den, DEN[:], [ev_w8])
            if stop_after in ("s4b", "s4b_scores", "s4b_t1", "s4b_mask"):
                continue
            ev_wbc_used = None
            for e in range(E):
                par = e % 2
                h_t = hA_t if par == 0 else hB_t
                ev_bcw = None
                for n in range(nn):
                    te.wait(ev_w8)
                    te.wait(pfree["PD"])
                    te.wait(ev_wbc_used)
                    ev_t = E_te(lambda q, e=e, n=n, nsz=nsz:
                                q.matmul(PD[:, 0:nsz], OH[:, e * P:(e + 1) * P],
                                         W8[:, n * nsz:(n + 1) * nsz], start=True, stop=True),
                                mark=True)
                    dv.wait(ev_t)
                    ev_bcw = E_dv(lambda q, n=n, nsz=nsz:
                                  q.tensor_copy(WBC[:, n * nsz:(n + 1) * nsz], PD[:, 0:nsz]),
                                  mark=True)
                    pfree["PD"] = ev_bcw
                if stop_after in ("s5_wbc", "s5_wbc_nomm", "s5_wbc_noev", "s5_wbc_pb"):
                    break
                ev_h = [None] * FT
                ev_t3 = None
                for mf in range(FT):
                    blk1, s1 = wload(w1_h[e, mf])
                    te.wait(wslot_ld[s1])
                    te.wait(pfree["PA"])
                    last1 = mm_block(blk1, lambda k: xb_t(k), PA, Cc)
                    wslot_free[s1] = last1
                    blk3, s3 = wload(w3_h[e, mf])
                    te.wait(wslot_ld[s3])
                    te.wait(pfree["PF"])
                    last3 = mm_block(blk3, lambda k: xb_t(k), PF, Cc)
                    wslot_free[s3] = last3
                    ac.wait(last1)
                    ac.wait(ev_hfree[par])
                    if par == 0:
                        ac.wait(ev_sg_free)
                        ac.wait(ev_rowS)
                    ev_s = E_ac(lambda q, mf=mf, h_t=h_t, cc=Cc:
                                q.activation(h_t(mf)[:, 0:cc], PA[:, 0:cc], AF.Silu),
                                mark=True)
                    pfree["PA"] = ev_s
                    dv.wait(last3)
                    dv.wait(ev_bcw)
                    ev_t3 = E_dv(lambda q, cc=Cc:
                                 q.tensor_mul(eph_t(5)[:, 0:cc], PF[:, 0:cc], WBC[:, 0:cc]),
                                 mark=True)
                    pfree["PF"] = ev_t3
                    dv.wait(ev_s)
                    ev_h[mf] = E_dv(lambda q, mf=mf, h_t=h_t, cc=Cc:
                                    q.tensor_mul(h_t(mf)[:, 0:cc], h_t(mf)[:, 0:cc], eph_t(5)[:, 0:cc]),
                                    mark=True)
                ev_wbc_used = ev_t3
                if stop_after == "s5_h":
                    break
                last2 = None
                for m in range(KT):
                    blk2, s2 = wload(w2_h[e, m])
                    te.wait(wslot_ld[s2])
                    te.wait(pfree["PB"])
                    for k in range(KT):
                        te.wait(ev_h[k])
                    last2 = mm_block(blk2, lambda k, h_t=h_t: h_t(k), PB, Cc)
                    wslot_free[s2] = last2
                    dv.wait(last2)
                    dv.wait(ev_x1[m] if e == 0 else ev_y2[m])
                    ev_y2[m] = E_dv(lambda q, m=m, cc=Cc:
                                    q.tensor_add(y_t(m)[:, 0:cc], PB[:, 0:cc], y_t(m)[:, 0:cc]),
                                    mark=True)
                    pfree["PB"] = ev_y2[m]
                ev_hfree[par] = last2
                if stop_after == "s5_e0":
                    break

            if dbg and last_real:
                ev_d = dump(dbg_y2, GV[:], [e_ for e_ in ev_y2])
                ac.wait(ev_d)
            if stop_after == "s5":
                continue
            if stop_after in ("s5", "s5_wbc", "s5_h", "s5_e0",
                              "s5_wbc_nomm", "s5_wbc_noev", "s5_wbc_pb"):
                continue
            ev_rowS2 = None
            for m in range(KT):
                ac.wait(ev_y2[m])
                ac.wait(ev_hfree[0])
                ev_yb2 = E_ac(lambda q, m=m, cc=Cc:
                              q.copy(sg_t(m)[:, 0:cc], y_t(m)[:, 0:cc]),
                              mark=True)
                sl = 3 + (m % 2)
                ac.wait(ev_eph34[m % 2])
                ev_sq2 = E_ac(lambda q, m=m, sl=sl, cc=Cc:
                              q.activation(eph_t(sl)[:, 0:cc], y_t(m)[:, 0:cc], AF.Square),
                              mark=True)
                te.wait(ev_yb2)
                te.wait(ev_sq2)
                if m == 0:
                    te.wait(pfree["PC"])
                    te.wait(pfree["PD"])
                evk = None
                for n in range(nn):
                    rS = 0 if n == 0 else 32
                    psQ = PC[64:65, 0:nsz] if n == 0 else PD[0:1, 0:nsz]
                    E_te(lambda q, m=m, n=n, rS=rS, nsz=nsz, st=(m == 0), sp=(m == KT - 1):
                         q.matmul(PC[rS:rS + 1, 0:nsz], ONB[:, :],
                                  sg_t(m)[:, n * nsz:(n + 1) * nsz], start=st, stop=sp))
                    evk = E_te(lambda q, sl=sl, n=n, psQ=psQ, nsz=nsz, st=(m == 0), sp=(m == KT - 1):
                               q.matmul(psQ, ONB[:, :],
                                        eph_t(sl)[:, n * nsz:(n + 1) * nsz], start=st, stop=sp),
                               mark=(n == nn - 1))
                ev_eph34[m % 2] = evk
                if m == KT - 1:
                    ev_rowS2 = evk
            ev_sg_free_new = ev_rowS2
            ev_bc2, ev_pc2 = ln_rows(ev_rowS2)
            pfree["PC"] = ev_pc2
            ev_out_dma_hi = None
            for m in range(KT):
                dv.wait(ev_bc2)
                E_dv(lambda q, m=m, cc=Cc:
                     q.tensor_mul(out_t(m)[:, 0:cc], y_t(m)[:, 0:cc], BC[:, 0:cc]))
                ev_o = E_dv(lambda q, m=m, cc=Cc:
                            q.tensor_add(out_t(m)[:, 0:cc], out_t(m)[:, 0:cc], BC[:, C:C + cc]),
                            mark=True)
                gp.wait(ev_o)
                ev_dma = E_gp(lambda q, m=m, t0=tok0, cc=Cc:
                              q.dma_start(out=out_h[m * P:(m + 1) * P, t0 - W:t0 - W + cc],
                                          in_=out_t(m)[:, 0:cc]),
                              "out")
                if m >= 8:
                    ev_out_dma_hi = ev_dma
                if m == KT - 1:
                    ev_gv_free = ev_o
            ev_t2_dma = ev_out_dma_hi
            ev_sg_free = ev_sg_free_new

        for w_ in ("wt0", "wt1", "xb", "out", "x", "cn"):
            gp.wait((sems[w_], cnt[w_]))
        gp.wait((s_dv, cnt["dv"]))
        gp.wait((s_ac, cnt["ac"]))
        gp.wait((s_pe, cnt["pe"]))

        block.sync(lambda q: sy.replay(q))
        block.gpsimd(lambda q: gp.replay(q))
        block.tensor(lambda q: te.replay(q))
        block.vector(lambda q: dv.replay(q))
        block.scalar(lambda q: ac.replay(q))
    return nc


def _prep_shared(Wi, Wf, Wg, Wo, gnorm, gate_w, w1, w3, w2):
    f32 = lambda a: np.ascontiguousarray(np.asarray(a, dtype=np.float32))
    sh = {}
    sh["wi_blk"] = _wblocks(f32(Wi))
    sh["wf_blk"] = _wblocks(f32(Wf))
    sh["wg_blk"] = _wblocks(f32(Wg))
    sh["wo_blk"] = _wblocks(f32(Wo))
    w1, w3, w2 = f32(w1), f32(w3), f32(w2)
    sh["w1_blk"] = np.ascontiguousarray(np.stack([_wblocks(w1[e]) for e in range(E)]))
    sh["w3_blk"] = np.ascontiguousarray(np.stack([_wblocks(w3[e]) for e in range(E)]))
    sh["w2_blk"] = np.ascontiguousarray(np.stack([_wblocks(w2[e]) for e in range(E)]))
    sh["gw_t"] = np.ascontiguousarray(f32(gate_w).T.reshape(KT, P, E))
    sh["gn_t"] = np.ascontiguousarray(f32(gnorm).reshape(KT, P).T)
    sh["idt"] = np.eye(P, dtype=np.float32)
    sh["idtb"] = np.eye(P, dtype=np.float32)
    oh = np.zeros((E, E * P), np.float32)
    for e in range(E):
        oh[e, e * P:(e + 1) * P] = 1.0
    sh["oh"] = oh
    sh["on1"] = np.ones((1, P), np.float32)
    sh["epsc"] = np.full((1, 1), 1e-5, np.float32)
    sh["onb"] = np.ones((P, 1), np.float32)
    return sh


def run_module(x, Wi, Wf, Wg, Wo, gnorm, gate_w, w1, w3, w2, W=512, C=1024, dbg=False, stop_after=None):
    x = np.asarray(x, dtype=np.float32)
    B, T, _ = x.shape
    per_b = NCORES // B
    T_core = T // per_b
    sh = _prep_shared(Wi, Wf, Wg, Wo, gnorm, gate_w, w1, w3, w2)
    ims = []
    for cid in range(NCORES):
        b, q = cid // per_b, cid % per_b
        t0 = q * T_core
        warm = np.zeros((W, D), np.float32) if t0 == 0 else x[b, t0 - W:t0]
        xc = np.concatenate([warm, x[b, t0:t0 + T_core]], axis=0)
        im = dict(sh)
        im["x_fm"] = np.ascontiguousarray(xc.T)
        ims.append(im)
    nc = build_nc(T_core, W, C, dbg=dbg, stop_after=stop_after)
    res = run_bass_kernel_spmd(nc, ims, core_ids=list(range(NCORES)))
    out = np.empty((B, T, D), np.float32)
    for cid in range(NCORES):
        b, q = cid // per_b, cid % per_b
        out[b, q * T_core:(q + 1) * T_core] = np.asarray(res.results[cid]["out_fm"]).T
    if dbg:
        return out, res.results
    return out


def kernel(L0_attn_hidden_states, L0_attn_Wi, L0_attn_Wf, L0_attn_Wg, L0_attn_Wo,
           L0_attn_gnorm_w, L0_ffn_gate_w, L0_ffn_w1, L0_ffn_w3, L0_ffn_w2):
    return run_module(L0_attn_hidden_states, L0_attn_Wi, L0_attn_Wf, L0_attn_Wg,
                      L0_attn_Wo, L0_attn_gnorm_w, L0_ffn_gate_w, L0_ffn_w1,
                      L0_ffn_w3, L0_ffn_w2, W=512, C=1024)


# version 04
# speedup vs baseline: 1.0823x; 1.0132x over previous
from contextlib import ExitStack
import numpy as np
import ml_dtypes
import concourse.bass as bass
import concourse.mybir as mybir
from concourse.bass_utils import run_bass_kernel_spmd

F32 = mybir.dt.float32
BF16 = mybir.dt.bfloat16
AF = mybir.ActivationFunctionType
ALU = mybir.AluOpType
AX = mybir.AxisListType.X

P = 128
D = 2048
KT = D // P
E = 8
FT = 2048 // P
EPS = 1e-5
NEG_BIG = -1e30
NCORES = 8


def _wblocks(Wmat):
    M = Wmat.shape[0] // P
    K = Wmat.shape[1] // P
    t = Wmat.reshape(M, P, K, P)
    return np.ascontiguousarray(t.transpose(0, 3, 2, 1)).reshape(M, P, K * P)


class _Eng:
    def __init__(self):
        self.thunks = []
        self.waited = {}

    def wait(self, ev):
        if ev is None:
            return
        sem, v = ev
        if v <= 0 or self.waited.get(id(sem), 0) >= v:
            return
        self.waited[id(sem)] = v
        self.thunks.append(("w", (sem, v)))

    def replay(self, q):
        for kind, x in self.thunks:
            if kind == "w":
                q.wait_ge(x[0], x[1])
            else:
                x(q)


def build_nc(T_core, W, C, dbg=False, stop_after=None):
    assert T_core % C == 0 and W <= C
    NSZ = min(512, C)
    NS = C // NSZ
    JT = C // P
    T_all = W + T_core
    NWB = 2

    nc = bass.Bass("TRN2", target_bir_lowering=False)
    ap = lambda t: t.ap()

    x_h = ap(nc.dram_tensor("x_fm", [D, T_all], F32, kind="ExternalInput"))
    wi_h = ap(nc.dram_tensor("wi_blk", [KT, P, D], F32, kind="ExternalInput"))
    wf_h = ap(nc.dram_tensor("wf_blk", [KT, P, D], F32, kind="ExternalInput"))
    wg_h = ap(nc.dram_tensor("wg_blk", [KT, P, D], F32, kind="ExternalInput"))
    wo_h = ap(nc.dram_tensor("wo_blk", [KT, P, D], F32, kind="ExternalInput"))
    w1_h = ap(nc.dram_tensor("w1_blk", [E, FT, P, D], F32, kind="ExternalInput"))
    w3_h = ap(nc.dram_tensor("w3_blk", [E, FT, P, D], F32, kind="ExternalInput"))
    w2_h = ap(nc.dram_tensor("w2_blk", [E, KT, P, 2048], F32, kind="ExternalInput"))
    gw_h = ap(nc.dram_tensor("gw_t", [KT, P, E], F32, kind="ExternalInput"))
    gn_h = ap(nc.dram_tensor("gn_t", [P, KT], F32, kind="ExternalInput"))
    idt_h = ap(nc.dram_tensor("idt", [P, P], F32, kind="ExternalInput"))
    idtb_h = ap(nc.dram_tensor("idtb", [P, P], F32, kind="ExternalInput"))
    oh_h = ap(nc.dram_tensor("oh", [E, E * P], F32, kind="ExternalInput"))
    on1_h = ap(nc.dram_tensor("on1", [1, P], F32, kind="ExternalInput"))
    onb_h = ap(nc.dram_tensor("onb", [P, 1], F32, kind="ExternalInput"))
    eps_h = ap(nc.dram_tensor("epsc", [1, 1], F32, kind="ExternalInput"))
    out_h = ap(nc.dram_tensor("out_fm", [D, T_core], F32, kind="ExternalOutput"))
    if dbg:
        dbg_o = ap(nc.dram_tensor("dbg_o", [P, KT * C], F32, kind="ExternalOutput"))
        dbg_v = ap(nc.dram_tensor("dbg_v", [P, KT * C // 2], F32, kind="ExternalOutput"))
        dbg_gate = ap(nc.dram_tensor("dbg_gate", [P, 8 * C], F32, kind="ExternalOutput"))
        dbg_o2 = ap(nc.dram_tensor("dbg_o2", [P, KT * C], F32, kind="ExternalOutput"))
        dbg_y = ap(nc.dram_tensor("dbg_y", [P, KT * C], F32, kind="ExternalOutput"))
        dbg_x1 = ap(nc.dram_tensor("dbg_x1", [P, KT * C], F32, kind="ExternalOutput"))
        dbg_w8 = ap(nc.dram_tensor("dbg_w8", [E, C], F32, kind="ExternalOutput"))
        dbg_y2 = ap(nc.dram_tensor("dbg_y2", [P, KT * C], F32, kind="ExternalOutput"))
        dbg_st = ap(nc.dram_tensor("dbg_st", [P, KT], F32, kind="ExternalOutput"))
        dbg_lt = ap(nc.dram_tensor("dbg_lt", [P, (C // P) * E], F32, kind="ExternalOutput"))
        dbg_et = ap(nc.dram_tensor("dbg_et", [P, (C // P) * E], F32, kind="ExternalOutput"))
        dbg_den = ap(nc.dram_tensor("dbg_den", [P, C // P], F32, kind="ExternalOutput"))
        dbg_msk = ap(nc.dram_tensor("dbg_msk", [P, (C // P) * E], F32, kind="ExternalOutput"))

    with ExitStack() as _stack:
        en = _stack.enter_context
        GV = en(nc.sbuf_tensor("GV", [P, KT * C], F32))
        T2 = en(nc.sbuf_tensor("T2", [P, KT * C // 2], F32))
        XB = en(nc.sbuf_tensor("XB", [P, KT * C], BF16))
        SG = en(nc.sbuf_tensor("SG", [P, KT * C], BF16))
        EPH = en(nc.sbuf_tensor("EPH", [P, 6 * C], BF16))
        WBUF = en(nc.sbuf_tensor("WBUF", [P, NWB * D], BF16))
        BC = en(nc.sbuf_tensor("BC", [P, 2 * C], F32))
        AB = en(nc.sbuf_tensor("AB", [1, 2 * NSZ], F32))
        MRow = en(nc.sbuf_tensor("MRow", [1, NSZ], BF16))
        WBC = en(nc.sbuf_tensor("WBC", [P, C], BF16))
        L8 = en(nc.sbuf_tensor("L8", [E, C], BF16))
        W8 = en(nc.sbuf_tensor("W8", [E, C], BF16))
        LT = en(nc.sbuf_tensor("LT", [P, JT * E], BF16))
        ET = en(nc.sbuf_tensor("ET", [P, JT * E], F32))
        MSK = en(nc.sbuf_tensor("MSK", [P, JT * E], F32))
        M1 = en(nc.sbuf_tensor("M1", [P, JT], F32))
        M1N = en(nc.sbuf_tensor("M1N", [P, JT], F32))
        M2 = en(nc.sbuf_tensor("M2", [P, JT], F32))
        DEN = en(nc.sbuf_tensor("DEN", [P, JT], F32))
        STATE = en(nc.sbuf_tensor("STATE", [P, KT], F32))
        GN = en(nc.sbuf_tensor("GN", [P, KT], F32))
        IDTB = en(nc.sbuf_tensor("IDTB", [P, P], BF16))
        OH = en(nc.sbuf_tensor("OH", [E, E * P], BF16))
        ON1 = en(nc.sbuf_tensor("ON1", [1, P], F32))
        ONB = en(nc.sbuf_tensor("ONB", [P, 1], BF16))
        GWT = en(nc.sbuf_tensor("GWT", [P, KT * E], BF16))
        EPSC = en(nc.sbuf_tensor("EPSC", [1, 1], F32))
        PA = en(nc.psum_tensor("PA", [P, 1024], F32))
        PF = en(nc.psum_tensor("PF", [P, 1024], F32))
        PB = en(nc.psum_tensor("PB", [P, 1024], F32))
        PC = en(nc.psum_tensor("PC", [P, 512], F32))
        PD = en(nc.psum_tensor("PD", [P, 512], F32))
        s_pe = en(nc.semaphore("s_pe"))
        s_dv = en(nc.semaphore("s_dv"))
        s_ac = en(nc.semaphore("s_ac"))
        s_wt0 = en(nc.semaphore("s_wt0"))
        s_wt1 = en(nc.semaphore("s_wt1"))
        s_xb = en(nc.semaphore("s_xb"))
        s_out = en(nc.semaphore("s_out"))
        s_x = en(nc.semaphore("s_x"))
        s_cn = en(nc.semaphore("s_cn"))
        block = en(nc.Block())
        GVb = GV.bitcast(BF16)
        T2b = T2.bitcast(BF16)
        XBf = XB.bitcast(F32)
        PDb = PD.bitcast(BF16)

        te, dv, ac, gp, sy = _Eng(), _Eng(), _Eng(), _Eng(), _Eng()
        cnt = {"pe": 0, "dv": 0, "ac": 0, "wt0": 0, "wt1": 0, "xb": 0, "out": 0, "x": 0, "cn": 0}
        sems = {"pe": s_pe, "dv": s_dv, "ac": s_ac, "wt0": s_wt0, "wt1": s_wt1,
                "xb": s_xb, "out": s_out, "x": s_x, "cn": s_cn}
        engs = {"pe": te, "dv": dv, "ac": ac}
        step = {"pe": 1, "dv": 1, "ac": 1, "wt0": 16, "wt1": 16, "xb": 16, "out": 16, "x": 16, "cn": 16}

        def emit(which, fn, mark=False, eng=None):
            eng = eng if eng is not None else engs[which]
            if which not in ("dv", "ac") and not mark:
                eng.thunks.append(("i", fn))
                return None
            sem = sems[which]
            st = step[which]
            def g(q, fn=fn, sem=sem, st=st):
                fn(q).then_inc(sem, st)
            eng.thunks.append(("i", g))
            cnt[which] += st
            ev = (sem, cnt[which])
            if which in ("dv", "ac"):
                eng.thunks.append(("w", ev))
                eng.waited[id(sem)] = cnt[which]
            return ev

        E_te = lambda fn, mark=False: emit("pe", fn, mark)

        def dvs(fn):
            ev = emit("dv", fn, True)
            dv.wait(ev)
            return ev
        E_dv = lambda fn, mark=False: emit("dv", fn, mark)
        E_ac = lambda fn, mark=False: emit("ac", fn, mark)
        E_gp = lambda fn, cls: emit(cls, fn, mark=True, eng=gp)
        E_sy = lambda fn, cls: emit(cls, fn, mark=True, eng=sy)

        def gate_t(mi):
            return GV[:, (8 + mi) * C:(9 + mi) * C]

        def o_t(k):
            return GVb[:, k * C:(k + 1) * C]

        def y_t(m):
            return GV[:, m * C:(m + 1) * C]

        def v_t(mi):
            return T2b[:, mi * C:(mi + 1) * C]

        def o2_t(k):
            return T2b[:, k * C:(k + 1) * C]

        def hB_t(k):
            return T2b[:, k * C:(k + 1) * C]

        def hA_t(k):
            return SG[:, k * C:(k + 1) * C]

        def sg_t(k):
            return SG[:, k * C:(k + 1) * C]

        def xb_t(k):
            return XB[:, k * C:(k + 1) * C]

        def eph_t(i):
            return EPH[:, i * C:(i + 1) * C]

        def out_t(m):
            return XBf[:, m * C:(m + 1) * C] if m < 8 else T2[:, (m - 8) * C:(m - 7) * C]

        wslot_free = [None] * NWB
        wslot_ld = [None] * NWB
        wb_i = [0]

        def dump(dst_ap, src_ap, evs):
            for e_ in evs:
                gp.wait(e_)
            return E_gp(lambda q, d=dst_ap, s=src_ap: q.dma_start(out=d, in_=s), "xb")

        def wload(src):
            s = wb_i[0] % NWB
            gp.wait(wslot_free[s])
            dst = WBUF[:, s * D:(s + 1) * D]
            ev = E_gp(lambda q, dst=dst, src=src: q.dma_start(out=dst, in_=src),
                      "wt0" if s == 0 else "wt1")
            wslot_ld[s] = ev
            wb_i[0] += 1
            return dst, s

        def mm_block(blk, rhs_t, ps, Cc, mark_each=False):
            nsz = min(512, Cc)
            nn = Cc // nsz
            last = None
            for k in range(KT):
                for n in range(nn):
                    mark = (k == KT - 1 and n == nn - 1)
                    ev = E_te(lambda q, blk=blk, ps=ps, k=k, n=n, nsz=nsz,
                              st=(k == 0), sp=(k == KT - 1), rhs_t=rhs_t:
                              q.matmul(ps[:, n * nsz:(n + 1) * nsz],
                                       blk[:, k * P:(k + 1) * P],
                                       rhs_t(k)[:, n * nsz:(n + 1) * nsz],
                                       start=st, stop=sp),
                              mark=mark)
                    if mark:
                        last = ev
            return last

        pfree = {"PA": None, "PF": None, "PB": None, "PC": None, "PD": None}
        ev_gate_free = [None] * 8
        ev_gv_free = None
        ev_sg_free = None
        ev_t2_dma = None
        ev_hfree = [None, None]
        ev_xb_free = None
        ev_eph34 = [None, None]
        ev_eph012 = [None, None, None]
        ev_state0 = None

        ev_const = None
        for dst, src in ((GN[:], gn_h), (ON1[:], on1_h),
                         (OH[:], oh_h), (ONB[:], onb_h), (IDTB[:], idtb_h),
                         (GWT[:].rearrange("p (k e) -> p k e", k=KT),
                          gw_h.rearrange("k p e -> p k e")),
                         (EPSC[:], eps_h)):
            ev_const = E_gp(lambda q, d=dst, s=src: q.dma_start(out=d, in_=s), "cn")
        ev_state0 = E_dv(lambda q: q.memset(STATE[:], 0.0), mark=True)
        ev_out_all = None

        x_kpt = x_h.rearrange("(k p) t -> p k t", p=P)

        nchunks = T_core // C
        chunks = [("warm", 0, W)] + [("real", W + i * C, C) for i in range(nchunks)]
        ev_y2 = [None] * KT

        for ci, (kind, tok0, Cc) in enumerate(chunks):
            warm = kind == "warm"
            nsz = min(512, Cc)
            nn = Cc // nsz
            jt = Cc // P

            gp.wait(ev_const)
            gp.wait(ev_xb_free)
            gp.wait(ev_out_all)
            ev_xb = E_gp(lambda q, t0=tok0, cc=Cc:
                         q.dma_start(out=XB[:].rearrange("p (k c) -> p k c", k=KT)[:, :, 0:cc],
                                     in_=x_kpt[:, :, t0:t0 + cc]),
                         "xb")

            mats = [("i", wi_h), ("f", wf_h)] if warm else [("i", wi_h), ("f", wf_h), ("g", wg_h)]
            ev_scan = [None] * KT
            ev_sg = [None] * KT
            ev_o2 = [None] * KT
            for half in range(2):
                for mi in range(8):
                    m = half * 8 + mi
                    ev_mm = {}
                    for mat, hsrc in mats:
                        blk, slot = wload(hsrc[m])
                        ps = {"i": PA, "f": PF, "g": PB}[mat]
                        te.wait(wslot_ld[slot])
                        te.wait(ev_xb)
                        te.wait(pfree["PA" if mat == "i" else "PF" if mat == "f" else "PB"])
                        last = mm_block(blk, lambda k: xb_t(k), ps, Cc)
                        wslot_free[slot] = last
                        ev_mm[mat] = last
                    ac.wait(ev_mm["f"])
                    ac.wait(ev_gate_free[mi])
                    ac.wait(ev_gv_free)
                    ac.wait(ev_eph012[0])
                    ac.wait(ev_eph012[1])
                    ev_g1 = E_ac(lambda q, mi=mi, cc=Cc:
                                 q.activation(gate_t(mi)[:, 0:cc], PF[:, 0:cc], AF.Sigmoid),
                                 mark=True)
                    ev_e0 = E_ac(lambda q, cc=Cc:
                                 q.activation(eph_t(0)[:, 0:cc], PF[:, 0:cc], AF.Sigmoid, scale=-1.0),
                                 mark=True)
                    pfree["PF"] = ev_e0
                    ac.wait(ev_mm["i"])
                    ev_e1 = E_ac(lambda q, cc=Cc:
                                 q.activation(eph_t(1)[:, 0:cc], PA[:, 0:cc], AF.Silu),
                                 mark=True)
                    pfree["PA"] = ev_e1
                    dv.wait(ev_e0)
                    dv.wait(ev_e1)
                    dv.wait(ev_t2_dma)
                    ev_v = E_dv(lambda q, mi=mi, cc=Cc:
                                q.tensor_mul(v_t(mi)[:, 0:cc], eph_t(1)[:, 0:cc], eph_t(0)[:, 0:cc]),
                                mark=True)
                    ev_eph012[0] = ev_v
                    ev_eph012[1] = ev_v
                    if not warm:
                        ac.wait(ev_mm["g"])
                        ac.wait(ev_eph012[2])
                        ev_e2 = E_ac(lambda q, cc=Cc:
                                     q.activation(eph_t(2)[:, 0:cc], PB[:, 0:cc], AF.Silu),
                                     mark=True)
                        pfree["PB"] = ev_e2
                        dv.wait(ev_e2)
                        dv.wait(ev_sg_free)
                        dv.wait(ev_hfree[0])
                        ev_sg[m] = E_dv(lambda q, m=m, cc=Cc:
                                        q.tensor_single_scalar(sg_t(m)[:, 0:cc], eph_t(2)[:, 0:cc],
                                                               GN[:, m:m + 1], ALU.mult),
                                        mark=True)
                        ev_eph012[2] = ev_sg[m]
                    dv.wait(ev_g1)
                    dv.wait(ev_state0)
                    dvs(lambda q, m=m, mi=mi, cc=Cc:
                        q.tensor_tensor_scan(o_t(m)[:, 0:cc], gate_t(mi)[:, 0:cc],
                                             v_t(mi)[:, 0:cc], STATE[:, m:m + 1],
                                             ALU.mult, ALU.add))
                    ev_scan[m] = dvs(lambda q, m=m, cc=Cc:
                                     q.tensor_copy(STATE[:, m:m + 1], o_t(m)[:, cc - 1:cc]))
                    ev_gate_free[mi] = ev_scan[m]
            if warm:
                if dbg:
                    ev_d = dump(dbg_st, STATE[:], [ev_scan[KT - 1]])
                    dv.wait(ev_d)
                    if dbg == 2:
                        ev_d = dump(dbg_v, T2b[:, 0:KT * C // 2], [ev_scan[KT - 1]])
                        ev_d2 = dump(dbg_o, GVb[:, 0:KT * C], [ev_scan[KT - 1]])
                        ev_d3 = dump(dbg_gate, GV[:, 8 * C:KT * C], [ev_scan[KT - 1]])
                        dv.wait(ev_d)
                        dv.wait(ev_d2)
                        ac.wait(ev_d3)
                        sy.wait(ev_d3)
                continue
            last_real = (ci == len(chunks) - 1)
            if dbg == 2:
                last_real = False
            if dbg and last_real:
                ev_d = dump(dbg_v, T2b[:, 0:KT * C // 2], [ev_scan[KT - 1]])
                ev_d2 = dump(dbg_o, GVb[:, 0:KT * C], [ev_scan[KT - 1]])
                ev_d3 = dump(dbg_gate, GV[:, 8 * C:KT * C], [ev_scan[KT - 1]])
                dv.wait(ev_d)
                dv.wait(ev_d2)
                ac.wait(ev_d3)
                sy.wait(ev_d3)

            if stop_after == "s1":
                continue
            ev_row = None
            for k in range(KT):
                sl = 3 + (k % 2)
                ac.wait(ev_scan[k])
                ac.wait(ev_eph34[k % 2])
                ev_sq = E_ac(lambda q, k=k, sl=sl, cc=Cc:
                             q.activation(eph_t(sl)[:, 0:cc], o_t(k)[:, 0:cc], AF.Square),
                             mark=True)
                te.wait(ev_sq)
                if k == 0:
                    te.wait(pfree["PC"])
                evk = None
                for n in range(nn):
                    r = 0 if n == 0 else 32
                    evk = E_te(lambda q, sl=sl, n=n, r=r, nsz=nsz, st=(k == 0), sp=(k == KT - 1):
                               q.matmul(PC[r:r + 1, 0:nsz], ONB[:, :],
                                        eph_t(sl)[:, n * nsz:(n + 1) * nsz],
                                        start=st, stop=sp),
                               mark=(n == nn - 1))
                ev_eph34[k % 2] = evk
                if k == KT - 1:
                    ev_row = evk
            ev_bcr = None
            ev_pcr = None
            for n in range(nn):
                r = 0 if n == 0 else 32
                dv.wait(ev_row)
                ev_pcr = E_dv(lambda q, r=r, nsz=nsz:
                              q.tensor_copy(AB[0:1, 0:nsz], PC[r:r + 1, 0:nsz]),
                              mark=(n == nn - 1))
                ev_r = E_dv(lambda q, nsz=nsz: q.tensor_scalar_mul(AB[0:1, 0:nsz], AB[0:1, 0:nsz], 1.0 / D),
                            mark=True)
                ac.wait(ev_r)
                E_ac(lambda q, nsz=nsz: q.activation(AB[0:1, 0:nsz], AB[0:1, 0:nsz], AF.Ln, bias=EPSC[0:1, 0:1]))
                ev_rr = E_ac(lambda q, nsz=nsz:
                             q.activation(AB[0:1, 0:nsz], AB[0:1, 0:nsz], AF.Exp, scale=-0.5),
                             mark=True)
                te.wait(ev_rr)
                te.wait(pfree["PD"])
                ev_t = E_te(lambda q, nsz=nsz:
                            q.matmul(PD[:, 0:nsz], ON1[:, :], AB[0:1, 0:nsz],
                                     start=True, stop=True),
                            mark=True)
                dv.wait(ev_t)
                ev_bcr = E_dv(lambda q, n=n, nsz=nsz:
                              q.tensor_copy(BC[:, n * nsz:(n + 1) * nsz], PD[:, 0:nsz]),
                              mark=True)
                pfree["PD"] = ev_bcr
                ac.wait(ev_t)
                dv.wait(ev_t)
            pfree["PC"] = ev_pcr
            for k in range(KT):
                dv.wait(ev_bcr)
                dv.wait(ev_sg[k])
                E_dv(lambda q, k=k, cc=Cc:
                     q.tensor_mul(eph_t(5)[:, 0:cc], o_t(k)[:, 0:cc], BC[:, 0:cc]))
                ev_o2[k] = E_dv(lambda q, k=k, cc=Cc:
                                q.tensor_mul(o2_t(k)[:, 0:cc], eph_t(5)[:, 0:cc], sg_t(k)[:, 0:cc]),
                                mark=True)

            if dbg and last_real:
                ev_d = dump(dbg_o2, T2b[:, 0:KT * C], [e_ for e_ in ev_o2])
                dv.wait(ev_d)
                te.wait(ev_d)
            if stop_after == "s3":
                continue
            ev_y = [None] * KT
            ev_x1 = [None] * KT
            ev_x1b = [None] * KT
            ev_rowS = None
            for k in range(KT):
                sy.wait(ev_o2[k])
            sy.wait(ev_scan[KT - 1])
            ev_x = E_sy(lambda q, t0=tok0, cc=Cc:
                        q.dma_start(out=GV[:].rearrange("p (k c) -> p k c", k=KT)[:, :, 0:cc],
                                    in_=x_kpt[:, :, t0:t0 + cc]),
                        "x")
            for m in range(KT):
                pname = "PA" if m % 2 == 0 else "PF"
                ps = PA if m % 2 == 0 else PF
                blk, slot = wload(wo_h[m])
                te.wait(wslot_ld[slot])
                te.wait(pfree[pname])
                for k in range(KT):
                    te.wait(ev_o2[k])
                last = mm_block(blk, lambda k: o2_t(k), ps, Cc)
                wslot_free[slot] = last
                dv.wait(last)
                dv.wait(ev_x)
                ev_y[m] = E_dv(lambda q, m=m, ps=ps, cc=Cc:
                               q.tensor_add(y_t(m)[:, 0:cc], ps[:, 0:cc], y_t(m)[:, 0:cc]),
                               mark=True)
                pfree[pname] = ev_y[m]
                ac.wait(ev_y[m])
                ac.wait(ev_o2[m])
                ev_yb = E_ac(lambda q, m=m, cc=Cc:
                             q.copy(sg_t(m)[:, 0:cc], y_t(m)[:, 0:cc]),
                             mark=True)
                sl = 3 + (m % 2)
                ac.wait(ev_eph34[m % 2])
                ev_sqy = E_ac(lambda q, m=m, sl=sl, cc=Cc:
                              q.activation(eph_t(sl)[:, 0:cc], y_t(m)[:, 0:cc], AF.Square),
                              mark=True)
                te.wait(ev_yb)
                te.wait(ev_sqy)
                if m == 0:
                    te.wait(pfree["PC"])
                    te.wait(pfree["PD"])
                evk = None
                for n in range(nn):
                    rS = 0 if n == 0 else 32
                    psQ = PC[64:65, 0:nsz] if n == 0 else PD[0:1, 0:nsz]
                    E_te(lambda q, m=m, n=n, rS=rS, nsz=nsz, st=(m == 0), sp=(m == KT - 1):
                         q.matmul(PC[rS:rS + 1, 0:nsz], ONB[:, :],
                                  sg_t(m)[:, n * nsz:(n + 1) * nsz], start=st, stop=sp))
                    evk = E_te(lambda q, sl=sl, n=n, psQ=psQ, nsz=nsz, st=(m == 0), sp=(m == KT - 1):
                               q.matmul(psQ, ONB[:, :],
                                        eph_t(sl)[:, n * nsz:(n + 1) * nsz], start=st, stop=sp),
                               mark=(n == nn - 1))
                ev_eph34[m % 2] = evk
                if m == KT - 1:
                    ev_rowS = evk

            def ln_rows(ev_rows):
                ev_bc = None
                ev_pc_read = None
                for n in reversed(range(nn)):
                    rS = 0 if n == 0 else 32
                    psQ = PC[64:65, 0:nsz] if n == 0 else PD[0:1, 0:nsz]
                    dv.wait(ev_rows)
                    E_dv(lambda q, psQ=psQ, nsz=nsz:
                         q.tensor_copy(AB[0:1, 0:nsz], psQ))
                    ev_pc_read = E_dv(lambda q, rS=rS, nsz=nsz:
                                      q.tensor_copy(AB[0:1, nsz:2 * nsz],
                                                    PC[rS:rS + 1, 0:nsz]),
                                      mark=True)
                    if n == nn - 1 and nn > 1:
                        pfree["PD"] = ev_pc_read
                    E_dv(lambda q, nsz=nsz:
                         q.tensor_scalar_mul(AB[0:1, 0:nsz], AB[0:1, 0:nsz], 1.0 / D))
                    E_dv(lambda q, nsz=nsz:
                         q.tensor_scalar_mul(AB[0:1, nsz:2 * nsz], AB[0:1, nsz:2 * nsz], 1.0 / D))
                    E_dv(lambda q, nsz=nsz:
                         q.tensor_mul(MRow[0:1, 0:nsz],
                                      AB[0:1, nsz:2 * nsz], AB[0:1, nsz:2 * nsz]))
                    ev_var = E_dv(lambda q, nsz=nsz:
                                  q.tensor_sub(AB[0:1, 0:nsz], AB[0:1, 0:nsz],
                                               MRow[0:1, 0:nsz]),
                                  mark=True)
                    ac.wait(ev_var)
                    E_ac(lambda q, nsz=nsz:
                         q.activation(AB[0:1, 0:nsz], AB[0:1, 0:nsz], AF.Ln, bias=EPSC[0:1, 0:1]))
                    ev_A = E_ac(lambda q, nsz=nsz:
                                q.activation(AB[0:1, 0:nsz], AB[0:1, 0:nsz], AF.Exp, scale=-0.5),
                                mark=True)
                    dv.wait(ev_A)
                    E_dv(lambda q, nsz=nsz:
                         q.tensor_mul(AB[0:1, nsz:2 * nsz], AB[0:1, nsz:2 * nsz],
                                      AB[0:1, 0:nsz]))
                    ev_B = E_dv(lambda q, nsz=nsz:
                                q.tensor_scalar_mul(AB[0:1, nsz:2 * nsz],
                                                    AB[0:1, nsz:2 * nsz], -1.0),
                                mark=True)
                    for off_ab, off_bc, evr in ((0, 0, ev_A), (nsz, C, ev_B)):
                        te.wait(evr)
                        te.wait(pfree["PD"])
                        ev_t = E_te(lambda q, off_ab=off_ab, nsz=nsz:
                                    q.matmul(PD[:, 0:nsz], ON1[:, :],
                                             AB[0:1, off_ab:off_ab + nsz],
                                             start=True, stop=True),
                                    mark=True)
                        dv.wait(ev_t)
                        ev_bc = E_dv(lambda q, off_bc=off_bc, n=n, nsz=nsz:
                                     q.tensor_copy(BC[:, off_bc + n * nsz:off_bc + (n + 1) * nsz],
                                                   PD[:, 0:nsz]),
                                     mark=True)
                        pfree["PD"] = ev_bc
                    ac.wait(ev_t)
                    dv.wait(ev_t)
                return ev_bc, ev_pc_read

            if dbg and last_real:
                ev_d = dump(dbg_y, GV[:], [e_ for e_ in ev_y])
                dv.wait(ev_d)
            ev_bc1, ev_pc1 = ln_rows(ev_rowS)
            pfree["PC"] = ev_pc1
            for m in range(KT):
                dv.wait(ev_bc1)
                E_dv(lambda q, m=m, cc=Cc:
                     q.tensor_mul(y_t(m)[:, 0:cc], y_t(m)[:, 0:cc], BC[:, 0:cc]))
                ev_x1[m] = E_dv(lambda q, m=m, cc=Cc:
                                q.tensor_add(y_t(m)[:, 0:cc], y_t(m)[:, 0:cc], BC[:, C:C + cc]),
                                mark=True)
                ac.wait(ev_x1[m])
                ev_x1b[m] = E_ac(lambda q, m=m, cc=Cc:
                                 q.copy(xb_t(m)[:, 0:cc], y_t(m)[:, 0:cc]),
                                 mark=True)

            if dbg and last_real:
                ev_d = dump(dbg_x1, GV[:], [e_ for e_ in ev_x1])
                dv.wait(ev_d)
                te.wait(ev_d)
            if stop_after == "s4":
                continue
            ev_l8 = None
            te.wait(pfree["PC"])
            for n in range(nn):
                for m in range(KT):
                    te.wait(ev_x1b[m])
                rb = 0 if n == 0 else 32
                last = None
                for k in range(KT):
                    last = E_te(lambda q, k=k, n=n, rb=rb, nsz=nsz, st=(k == 0), sp=(k == KT - 1):
                                q.matmul(PC[rb:rb + E, 0:nsz], GWT[:, k * E:(k + 1) * E],
                                         xb_t(k)[:, n * nsz:(n + 1) * nsz],
                                         start=st, stop=sp),
                                mark=(k == KT - 1))
                dv.wait(last)
                ev_l8 = E_dv(lambda q, n=n, rb=rb, nsz=nsz:
                             q.tensor_copy(L8[:, n * nsz:(n + 1) * nsz], PC[rb:rb + E, 0:nsz]),
                             mark=True)
            pfree["PC"] = ev_l8
            if stop_after == "s4b_scores":
                continue
            ev_w8 = None
            for j in range(jt):
                te.wait(ev_l8)
                te.wait(pfree["PD"])
                ev_t = E_te(lambda q, j=j:
                            q.transpose(PDb[:, 0:E], L8[:, j * P:(j + 1) * P], IDTB[0:E, 0:E]),
                            mark=True)
                dv.wait(ev_t)
                ev_lt = dvs(lambda q, j=j:
                            q.tensor_copy(LT[:, j * E:(j + 1) * E], PDb[:, 0:E]))
                pfree["PD"] = ev_lt
                if stop_after == "s4b_t1":
                    continue
                dvs(lambda q, j=j: q.tensor_reduce(M1[:, j:j + 1], LT[:, j * E:(j + 1) * E], AX, ALU.max))
                dvs(lambda q, j=j: q.tensor_scalar_mul(M1N[:, j:j + 1], M1[:, j:j + 1], -1.0))
                dvs(lambda q, j=j: q.tensor_single_scalar(MSK[:, j * E:(j + 1) * E],
                                                          LT[:, j * E:(j + 1) * E], M1[:, j:j + 1], ALU.is_ge))
                dvs(lambda q, j=j: q.scalar_tensor_tensor(MSK[:, j * E:(j + 1) * E],
                                                          MSK[:, j * E:(j + 1) * E], NEG_BIG,
                                                          LT[:, j * E:(j + 1) * E], ALU.mult, ALU.add))
                dvs(lambda q, j=j: q.tensor_reduce(M2[:, j:j + 1], MSK[:, j * E:(j + 1) * E], AX, ALU.max))
                ev_mk = dvs(lambda q, j=j: q.tensor_single_scalar(MSK[:, j * E:(j + 1) * E],
                                                                  LT[:, j * E:(j + 1) * E],
                                                                  M2[:, j:j + 1], ALU.is_ge))
                if stop_after == "s4b_mask":
                    continue
                if dbg and last_real and j == jt - 1:
                    ev_d = dump(dbg_lt, LT[:], [ev_mk])
                    ev_d2 = dump(dbg_et, MSK[:], [ev_mk])
                    ac.wait(ev_d)
                    ac.wait(ev_d2)
                    dv.wait(ev_d)
                    dv.wait(ev_d2)
                ac.wait(ev_mk)
                ev_ex = E_ac(lambda q, j=j:
                             q.activation(ET[:, j * E:(j + 1) * E], LT[:, j * E:(j + 1) * E], AF.Exp,
                                          bias=M1N[:, j:j + 1], accum_out=DEN[:, j:j + 1]),
                             mark=True)
                dv.wait(ev_ex)
                dvs(lambda q, j=j: q.reciprocal(DEN[:, j:j + 1], DEN[:, j:j + 1]))
                dvs(lambda q, j=j: q.tensor_mul(MSK[:, j * E:(j + 1) * E],
                                                MSK[:, j * E:(j + 1) * E], ET[:, j * E:(j + 1) * E]))
                dvs(lambda q, j=j: q.tensor_single_scalar(MSK[:, j * E:(j + 1) * E],
                                                          MSK[:, j * E:(j + 1) * E],
                                                          DEN[:, j:j + 1], ALU.mult))
                ev_c = dvs(lambda q, j=j: q.tensor_copy(LT[:, j * E:(j + 1) * E],
                                                        MSK[:, j * E:(j + 1) * E]))
                te.wait(ev_c)
                te.wait(pfree["PD"])
                ev_t2_ = E_te(lambda q, j=j:
                              q.transpose(PDb[0:E, 0:P], LT[:, j * E:(j + 1) * E], IDTB[:, :]),
                              mark=True)
                dv.wait(ev_t2_)
                ev_w8 = E_dv(lambda q, j=j:
                             q.tensor_copy(W8[:, j * P:(j + 1) * P], PDb[0:E, 0:P]),
                             mark=True)
                pfree["PD"] = ev_w8
                if dbg and last_real and j == jt - 1:
                    dump(dbg_msk, MSK[:], [ev_w8])

            if dbg and last_real:
                ev_d = dump(dbg_w8, W8[:], [ev_w8])
                te.wait(ev_d)
                dv.wait(ev_d)
                dump(dbg_den, DEN[:], [ev_w8])
            if stop_after in ("s4b", "s4b_scores", "s4b_t1", "s4b_mask"):
                continue
            ev_wbc_used = None
            for e in range(E):
                par = e % 2
                h_t = hA_t if par == 0 else hB_t
                ev_bcw = None
                for n in range(nn):
                    te.wait(ev_w8)
                    te.wait(pfree["PD"])
                    te.wait(ev_wbc_used)
                    ev_t = E_te(lambda q, e=e, n=n, nsz=nsz:
                                q.matmul(PD[:, 0:nsz], OH[:, e * P:(e + 1) * P],
                                         W8[:, n * nsz:(n + 1) * nsz], start=True, stop=True),
                                mark=True)
                    dv.wait(ev_t)
                    ev_bcw = E_dv(lambda q, n=n, nsz=nsz:
                                  q.tensor_copy(WBC[:, n * nsz:(n + 1) * nsz], PD[:, 0:nsz]),
                                  mark=True)
                    pfree["PD"] = ev_bcw
                if stop_after in ("s5_wbc", "s5_wbc_nomm", "s5_wbc_noev", "s5_wbc_pb"):
                    break
                ev_h = [None] * FT
                ev_t3 = None
                for mf in range(FT):
                    blk1, s1 = wload(w1_h[e, mf])
                    te.wait(wslot_ld[s1])
                    te.wait(pfree["PA"])
                    last1 = mm_block(blk1, lambda k: xb_t(k), PA, Cc)
                    wslot_free[s1] = last1
                    blk3, s3 = wload(w3_h[e, mf])
                    te.wait(wslot_ld[s3])
                    te.wait(pfree["PF"])
                    last3 = mm_block(blk3, lambda k: xb_t(k), PF, Cc)
                    wslot_free[s3] = last3
                    if e == E - 1 and mf == FT - 1:
                        ev_xb_free = last3
                    ac.wait(last1)
                    ac.wait(ev_hfree[par])
                    if par == 0:
                        ac.wait(ev_sg_free)
                        ac.wait(ev_rowS)
                    ev_s = E_ac(lambda q, mf=mf, h_t=h_t, cc=Cc:
                                q.activation(h_t(mf)[:, 0:cc], PA[:, 0:cc], AF.Silu),
                                mark=True)
                    pfree["PA"] = ev_s
                    dv.wait(last3)
                    dv.wait(ev_bcw)
                    ev_t3 = E_dv(lambda q, cc=Cc:
                                 q.tensor_mul(eph_t(5)[:, 0:cc], PF[:, 0:cc], WBC[:, 0:cc]),
                                 mark=True)
                    pfree["PF"] = ev_t3
                    dv.wait(ev_s)
                    ev_h[mf] = E_dv(lambda q, mf=mf, h_t=h_t, cc=Cc:
                                    q.tensor_mul(h_t(mf)[:, 0:cc], h_t(mf)[:, 0:cc], eph_t(5)[:, 0:cc]),
                                    mark=True)
                ev_wbc_used = ev_t3
                if stop_after == "s5_h":
                    break
                last2 = None
                for m in range(KT):
                    blk2, s2 = wload(w2_h[e, m])
                    te.wait(wslot_ld[s2])
                    te.wait(pfree["PB"])
                    for k in range(KT):
                        te.wait(ev_h[k])
                    last2 = mm_block(blk2, lambda k, h_t=h_t: h_t(k), PB, Cc)
                    wslot_free[s2] = last2
                    dv.wait(last2)
                    dv.wait(ev_x1[m] if e == 0 else ev_y2[m])
                    ev_y2[m] = E_dv(lambda q, m=m, cc=Cc:
                                    q.tensor_add(y_t(m)[:, 0:cc], PB[:, 0:cc], y_t(m)[:, 0:cc]),
                                    mark=True)
                    pfree["PB"] = ev_y2[m]
                ev_hfree[par] = last2
                if stop_after == "s5_e0":
                    break

            if dbg and last_real:
                ev_d = dump(dbg_y2, GV[:], [e_ for e_ in ev_y2])
                ac.wait(ev_d)
            if stop_after == "s5":
                continue
            if stop_after in ("s5", "s5_wbc", "s5_h", "s5_e0",
                              "s5_wbc_nomm", "s5_wbc_noev", "s5_wbc_pb"):
                continue
            ev_rowS2 = None
            for m in range(KT):
                ac.wait(ev_y2[m])
                ac.wait(ev_hfree[0])
                ev_yb2 = E_ac(lambda q, m=m, cc=Cc:
                              q.copy(sg_t(m)[:, 0:cc], y_t(m)[:, 0:cc]),
                              mark=True)
                sl = 3 + (m % 2)
                ac.wait(ev_eph34[m % 2])
                ev_sq2 = E_ac(lambda q, m=m, sl=sl, cc=Cc:
                              q.activation(eph_t(sl)[:, 0:cc], y_t(m)[:, 0:cc], AF.Square),
                              mark=True)
                te.wait(ev_yb2)
                te.wait(ev_sq2)
                if m == 0:
                    te.wait(pfree["PC"])
                    te.wait(pfree["PD"])
                evk = None
                for n in range(nn):
                    rS = 0 if n == 0 else 32
                    psQ = PC[64:65, 0:nsz] if n == 0 else PD[0:1, 0:nsz]
                    E_te(lambda q, m=m, n=n, rS=rS, nsz=nsz, st=(m == 0), sp=(m == KT - 1):
                         q.matmul(PC[rS:rS + 1, 0:nsz], ONB[:, :],
                                  sg_t(m)[:, n * nsz:(n + 1) * nsz], start=st, stop=sp))
                    evk = E_te(lambda q, sl=sl, n=n, psQ=psQ, nsz=nsz, st=(m == 0), sp=(m == KT - 1):
                               q.matmul(psQ, ONB[:, :],
                                        eph_t(sl)[:, n * nsz:(n + 1) * nsz], start=st, stop=sp),
                               mark=(n == nn - 1))
                ev_eph34[m % 2] = evk
                if m == KT - 1:
                    ev_rowS2 = evk
            ev_sg_free_new = ev_rowS2
            ev_bc2, ev_pc2 = ln_rows(ev_rowS2)
            pfree["PC"] = ev_pc2
            ev_out_dma_hi = None
            for m in range(KT):
                dv.wait(ev_bc2)
                E_dv(lambda q, m=m, cc=Cc:
                     q.tensor_mul(out_t(m)[:, 0:cc], y_t(m)[:, 0:cc], BC[:, 0:cc]))
                ev_o = E_dv(lambda q, m=m, cc=Cc:
                            q.tensor_add(out_t(m)[:, 0:cc], out_t(m)[:, 0:cc], BC[:, C:C + cc]),
                            mark=True)
                gp.wait(ev_o)
                ev_dma = E_gp(lambda q, m=m, t0=tok0, cc=Cc:
                              q.dma_start(out=out_h[m * P:(m + 1) * P, t0 - W:t0 - W + cc],
                                          in_=out_t(m)[:, 0:cc]),
                              "out")
                if m >= 8:
                    ev_out_dma_hi = ev_dma
                if m == KT - 1:
                    ev_gv_free = ev_o
            ev_t2_dma = ev_out_dma_hi
            ev_out_all = (sems["out"], cnt["out"])
            ev_sg_free = ev_sg_free_new

        for w_ in ("wt0", "wt1", "xb", "out", "x", "cn"):
            gp.wait((sems[w_], cnt[w_]))
        gp.wait((s_dv, cnt["dv"]))
        gp.wait((s_ac, cnt["ac"]))
        gp.wait((s_pe, cnt["pe"]))

        block.sync(lambda q: sy.replay(q))
        block.gpsimd(lambda q: gp.replay(q))
        block.tensor(lambda q: te.replay(q))
        block.vector(lambda q: dv.replay(q))
        block.scalar(lambda q: ac.replay(q))
    return nc


def _prep_shared(Wi, Wf, Wg, Wo, gnorm, gate_w, w1, w3, w2):
    f32 = lambda a: np.ascontiguousarray(np.asarray(a, dtype=np.float32))
    sh = {}
    sh["wi_blk"] = _wblocks(f32(Wi))
    sh["wf_blk"] = _wblocks(f32(Wf))
    sh["wg_blk"] = _wblocks(f32(Wg))
    sh["wo_blk"] = _wblocks(f32(Wo))
    w1, w3, w2 = f32(w1), f32(w3), f32(w2)
    sh["w1_blk"] = np.ascontiguousarray(np.stack([_wblocks(w1[e]) for e in range(E)]))
    sh["w3_blk"] = np.ascontiguousarray(np.stack([_wblocks(w3[e]) for e in range(E)]))
    sh["w2_blk"] = np.ascontiguousarray(np.stack([_wblocks(w2[e]) for e in range(E)]))
    sh["gw_t"] = np.ascontiguousarray(f32(gate_w).T.reshape(KT, P, E))
    sh["gn_t"] = np.ascontiguousarray(f32(gnorm).reshape(KT, P).T)
    sh["idt"] = np.eye(P, dtype=np.float32)
    sh["idtb"] = np.eye(P, dtype=np.float32)
    oh = np.zeros((E, E * P), np.float32)
    for e in range(E):
        oh[e, e * P:(e + 1) * P] = 1.0
    sh["oh"] = oh
    sh["on1"] = np.ones((1, P), np.float32)
    sh["epsc"] = np.full((1, 1), 1e-5, np.float32)
    sh["onb"] = np.ones((P, 1), np.float32)
    return sh


def run_module(x, Wi, Wf, Wg, Wo, gnorm, gate_w, w1, w3, w2, W=512, C=1024, dbg=False, stop_after=None):
    x = np.asarray(x, dtype=np.float32)
    B, T, _ = x.shape
    per_b = NCORES // B
    T_core = T // per_b
    sh = _prep_shared(Wi, Wf, Wg, Wo, gnorm, gate_w, w1, w3, w2)
    ims = []
    for cid in range(NCORES):
        b, q = cid // per_b, cid % per_b
        t0 = q * T_core
        warm = np.zeros((W, D), np.float32) if t0 == 0 else x[b, t0 - W:t0]
        xc = np.concatenate([warm, x[b, t0:t0 + T_core]], axis=0)
        im = dict(sh)
        im["x_fm"] = np.ascontiguousarray(xc.T)
        ims.append(im)
    nc = build_nc(T_core, W, C, dbg=dbg, stop_after=stop_after)
    res = run_bass_kernel_spmd(nc, ims, core_ids=list(range(NCORES)))
    out = np.empty((B, T, D), np.float32)
    for cid in range(NCORES):
        b, q = cid // per_b, cid % per_b
        out[b, q * T_core:(q + 1) * T_core] = np.asarray(res.results[cid]["out_fm"]).T
    if dbg:
        return out, res.results
    return out


def kernel(L0_attn_hidden_states, L0_attn_Wi, L0_attn_Wf, L0_attn_Wg, L0_attn_Wo,
           L0_attn_gnorm_w, L0_ffn_gate_w, L0_ffn_w1, L0_ffn_w3, L0_ffn_w2):
    return run_module(L0_attn_hidden_states, L0_attn_Wi, L0_attn_Wf, L0_attn_Wg,
                      L0_attn_Wo, L0_attn_gnorm_w, L0_ffn_gate_w, L0_ffn_w1,
                      L0_ffn_w3, L0_ffn_w2, W=512, C=1024)


# version 05
# speedup vs baseline: 1.1727x; 1.0835x over previous
from contextlib import ExitStack
import numpy as np
import ml_dtypes
import concourse.bass as bass
import concourse.mybir as mybir
from concourse.bass_utils import run_bass_kernel_spmd

F32 = mybir.dt.float32
BF16 = mybir.dt.bfloat16
AF = mybir.ActivationFunctionType
ALU = mybir.AluOpType
AX = mybir.AxisListType.X

P = 128
D = 2048
KT = D // P
E = 8
FT = 2048 // P
EPS = 1e-5
NEG_BIG = -1e30
NCORES = 8


def _wblocks(Wmat):
    M = Wmat.shape[0] // P
    K = Wmat.shape[1] // P
    t = Wmat.reshape(M, P, K, P)
    return np.ascontiguousarray(t.transpose(0, 3, 2, 1)).reshape(M, P, K * P)


class _Eng:
    def __init__(self):
        self.thunks = []
        self.waited = {}

    def wait(self, ev):
        if ev is None:
            return
        sem, v = ev
        if v <= 0 or self.waited.get(id(sem), 0) >= v:
            return
        self.waited[id(sem)] = v
        self.thunks.append(("w", (sem, v)))

    def replay(self, q):
        for kind, x in self.thunks:
            if kind == "w":
                q.wait_ge(x[0], x[1])
            else:
                x(q)


def build_nc(T_core, W, C, dbg=False, stop_after=None):
    assert T_core % C == 0 and W <= C
    NSZ = min(512, C)
    NS = C // NSZ
    JT = C // P
    T_all = W + T_core
    NWB = 2

    nc = bass.Bass("TRN2", target_bir_lowering=False)
    ap = lambda t: t.ap()

    x_h = ap(nc.dram_tensor("x_fm", [D, T_all], F32, kind="ExternalInput"))
    wi_h = ap(nc.dram_tensor("wi_blk", [KT, P, D], F32, kind="ExternalInput"))
    wf_h = ap(nc.dram_tensor("wf_blk", [KT, P, D], F32, kind="ExternalInput"))
    wg_h = ap(nc.dram_tensor("wg_blk", [KT, P, D], F32, kind="ExternalInput"))
    wo_h = ap(nc.dram_tensor("wo_blk", [KT, P, D], F32, kind="ExternalInput"))
    w1_h = ap(nc.dram_tensor("w1_blk", [E, FT, P, D], F32, kind="ExternalInput"))
    w3_h = ap(nc.dram_tensor("w3_blk", [E, FT, P, D], F32, kind="ExternalInput"))
    w2_h = ap(nc.dram_tensor("w2_blk", [E, KT, P, 2048], F32, kind="ExternalInput"))
    gw_h = ap(nc.dram_tensor("gw_t", [KT, P, E], F32, kind="ExternalInput"))
    gn_h = ap(nc.dram_tensor("gn_t", [P, KT], F32, kind="ExternalInput"))
    idt_h = ap(nc.dram_tensor("idt", [P, P], F32, kind="ExternalInput"))
    idtb_h = ap(nc.dram_tensor("idtb", [P, P], F32, kind="ExternalInput"))
    oh_h = ap(nc.dram_tensor("oh", [E, E * P], F32, kind="ExternalInput"))
    on1_h = ap(nc.dram_tensor("on1", [1, P], F32, kind="ExternalInput"))
    onb_h = ap(nc.dram_tensor("onb", [P, 1], F32, kind="ExternalInput"))
    eps_h = ap(nc.dram_tensor("epsc", [1, 1], F32, kind="ExternalInput"))
    out_h = ap(nc.dram_tensor("out_fm", [D, T_core], F32, kind="ExternalOutput"))
    if dbg:
        dbg_o = ap(nc.dram_tensor("dbg_o", [P, KT * C], F32, kind="ExternalOutput"))
        dbg_v = ap(nc.dram_tensor("dbg_v", [P, KT * C // 2], F32, kind="ExternalOutput"))
        dbg_gate = ap(nc.dram_tensor("dbg_gate", [P, 8 * C], F32, kind="ExternalOutput"))
        dbg_o2 = ap(nc.dram_tensor("dbg_o2", [P, KT * C], F32, kind="ExternalOutput"))
        dbg_y = ap(nc.dram_tensor("dbg_y", [P, KT * C], F32, kind="ExternalOutput"))
        dbg_x1 = ap(nc.dram_tensor("dbg_x1", [P, KT * C], F32, kind="ExternalOutput"))
        dbg_w8 = ap(nc.dram_tensor("dbg_w8", [E, C], F32, kind="ExternalOutput"))
        dbg_y2 = ap(nc.dram_tensor("dbg_y2", [P, KT * C], F32, kind="ExternalOutput"))
        dbg_st = ap(nc.dram_tensor("dbg_st", [P, KT], F32, kind="ExternalOutput"))
        dbg_lt = ap(nc.dram_tensor("dbg_lt", [P, (C // P) * E], F32, kind="ExternalOutput"))
        dbg_et = ap(nc.dram_tensor("dbg_et", [P, (C // P) * E], F32, kind="ExternalOutput"))
        dbg_den = ap(nc.dram_tensor("dbg_den", [P, C // P], F32, kind="ExternalOutput"))
        dbg_msk = ap(nc.dram_tensor("dbg_msk", [P, (C // P) * E], F32, kind="ExternalOutput"))

    with ExitStack() as _stack:
        en = _stack.enter_context
        GV = en(nc.sbuf_tensor("GV", [P, KT * C], F32))
        T2 = en(nc.sbuf_tensor("T2", [P, KT * C // 2], F32))
        XB = en(nc.sbuf_tensor("XB", [P, KT * C], BF16))
        SG = en(nc.sbuf_tensor("SG", [P, KT * C], BF16))
        EPH = en(nc.sbuf_tensor("EPH", [P, 6 * C], BF16))
        WBUF = en(nc.sbuf_tensor("WBUF", [P, NWB * D], BF16))
        BC = en(nc.sbuf_tensor("BC", [P, 2 * C], F32))
        AB = en(nc.sbuf_tensor("AB", [1, 2 * NSZ], F32))
        MRow = en(nc.sbuf_tensor("MRow", [1, NSZ], BF16))
        WBC = en(nc.sbuf_tensor("WBC", [P, C], BF16))
        L8 = en(nc.sbuf_tensor("L8", [E, C], BF16))
        W8 = en(nc.sbuf_tensor("W8", [E, C], BF16))
        LT = en(nc.sbuf_tensor("LT", [P, JT * E], BF16))
        ET = en(nc.sbuf_tensor("ET", [P, JT * E], F32))
        MSK = en(nc.sbuf_tensor("MSK", [P, JT * E], F32))
        M1 = en(nc.sbuf_tensor("M1", [P, JT], F32))
        M1N = en(nc.sbuf_tensor("M1N", [P, JT], F32))
        M2 = en(nc.sbuf_tensor("M2", [P, JT], F32))
        DEN = en(nc.sbuf_tensor("DEN", [P, JT], F32))
        STATE = en(nc.sbuf_tensor("STATE", [P, KT], F32))
        GN = en(nc.sbuf_tensor("GN", [P, KT], F32))
        IDTB = en(nc.sbuf_tensor("IDTB", [P, P], BF16))
        OH = en(nc.sbuf_tensor("OH", [E, E * P], BF16))
        ON1 = en(nc.sbuf_tensor("ON1", [1, P], F32))
        ONB = en(nc.sbuf_tensor("ONB", [P, 1], BF16))
        GWT = en(nc.sbuf_tensor("GWT", [P, KT * E], BF16))
        EPSC = en(nc.sbuf_tensor("EPSC", [1, 1], F32))
        PA = en(nc.psum_tensor("PA", [P, 1024], F32))
        PF = en(nc.psum_tensor("PF", [P, 1024], F32))
        PB = en(nc.psum_tensor("PB", [P, 1024], F32))
        PC = en(nc.psum_tensor("PC", [P, 512], F32))
        PD = en(nc.psum_tensor("PD", [P, 512], F32))
        s_pe = en(nc.semaphore("s_pe"))
        s_dv = en(nc.semaphore("s_dv"))
        s_ac = en(nc.semaphore("s_ac"))
        s_wt0 = en(nc.semaphore("s_wt0"))
        s_wt1 = en(nc.semaphore("s_wt1"))
        s_xb = en(nc.semaphore("s_xb"))
        s_out = en(nc.semaphore("s_out"))
        s_x = en(nc.semaphore("s_x"))
        s_cn = en(nc.semaphore("s_cn"))
        block = en(nc.Block())
        GVb = GV.bitcast(BF16)
        T2b = T2.bitcast(BF16)
        XBf = XB.bitcast(F32)
        PDb = PD.bitcast(BF16)

        te, dv, ac, gp, sy = _Eng(), _Eng(), _Eng(), _Eng(), _Eng()
        cnt = {"pe": 0, "dv": 0, "ac": 0, "wt0": 0, "wt1": 0, "xb": 0, "out": 0, "x": 0, "cn": 0}
        sems = {"pe": s_pe, "dv": s_dv, "ac": s_ac, "wt0": s_wt0, "wt1": s_wt1,
                "xb": s_xb, "out": s_out, "x": s_x, "cn": s_cn}
        engs = {"pe": te, "dv": dv, "ac": ac}
        step = {"pe": 1, "dv": 1, "ac": 1, "wt0": 16, "wt1": 16, "xb": 16, "out": 16, "x": 16, "cn": 16}

        def emit(which, fn, mark=False, eng=None):
            eng = eng if eng is not None else engs[which]
            if which not in ("dv", "ac") and not mark:
                eng.thunks.append(("i", fn))
                return None
            sem = sems[which]
            st = step[which]
            def g(q, fn=fn, sem=sem, st=st):
                fn(q).then_inc(sem, st)
            eng.thunks.append(("i", g))
            cnt[which] += st
            ev = (sem, cnt[which])
            if which in ("dv", "ac"):
                eng.thunks.append(("w", ev))
                eng.waited[id(sem)] = cnt[which]
            return ev

        E_te = lambda fn, mark=False: emit("pe", fn, mark)

        def dvs(fn):
            ev = emit("dv", fn, True)
            dv.wait(ev)
            return ev
        E_dv = lambda fn, mark=False: emit("dv", fn, mark)
        E_ac = lambda fn, mark=False: emit("ac", fn, mark)
        E_gp = lambda fn, cls: emit(cls, fn, mark=True, eng=gp)
        E_sy = lambda fn, cls: emit(cls, fn, mark=True, eng=sy)

        def gate_t(mi):
            return GV[:, (8 + mi) * C:(9 + mi) * C]

        def o_t(k):
            return GVb[:, k * C:(k + 1) * C]

        def y_t(m):
            return GV[:, m * C:(m + 1) * C]

        def v_t(mi):
            return T2b[:, mi * C:(mi + 1) * C]

        def o2_t(k):
            return T2b[:, k * C:(k + 1) * C]

        def hB_t(k):
            return T2b[:, k * C:(k + 1) * C]

        def hA_t(k):
            return SG[:, k * C:(k + 1) * C]

        def sg_t(k):
            return SG[:, k * C:(k + 1) * C]

        def xb_t(k):
            return XB[:, k * C:(k + 1) * C]

        def eph_t(i):
            return EPH[:, i * C:(i + 1) * C]

        def out_t(m):
            return XBf[:, m * C:(m + 1) * C] if m < 8 else T2[:, (m - 8) * C:(m - 7) * C]

        wslot_free = [None] * NWB
        wslot_ld = [None] * NWB
        wb_i = [0]

        def dump(dst_ap, src_ap, evs):
            for e_ in evs:
                gp.wait(e_)
            return E_gp(lambda q, d=dst_ap, s=src_ap: q.dma_start(out=d, in_=s), "xb")

        def wload(src):
            s = wb_i[0] % NWB
            gp.wait(wslot_free[s])
            dst = WBUF[:, s * D:(s + 1) * D]
            ev = E_gp(lambda q, dst=dst, src=src: q.dma_start(out=dst, in_=src),
                      "wt0" if s == 0 else "wt1")
            wslot_ld[s] = ev
            wb_i[0] += 1
            return dst, s

        def mm_block(blk, rhs_t, ps, Cc, mark_each=False):
            nsz = min(512, Cc)
            nn = Cc // nsz
            last = None
            for k in range(KT):
                for n in range(nn):
                    mark = (k == KT - 1 and n == nn - 1)
                    ev = E_te(lambda q, blk=blk, ps=ps, k=k, n=n, nsz=nsz,
                              st=(k == 0), sp=(k == KT - 1), rhs_t=rhs_t:
                              q.matmul(ps[:, n * nsz:(n + 1) * nsz],
                                       blk[:, k * P:(k + 1) * P],
                                       rhs_t(k)[:, n * nsz:(n + 1) * nsz],
                                       start=st, stop=sp),
                              mark=mark)
                    if mark:
                        last = ev
            return last

        pfree = {"PA": None, "PF": None, "PB": None, "PC": None, "PD": None}
        ev_gate_free = [None] * 8
        ev_gv_free = None
        ev_sg_free = None
        ev_t2_dma = None
        ev_hfree = [None, None]
        ev_xb_free = None
        ev_eph34 = [None, None]
        ev_eph012 = [None, None, None]
        ev_state0 = None

        ev_const = None
        for dst, src in ((GN[:], gn_h), (ON1[:], on1_h),
                         (OH[:], oh_h), (ONB[:], onb_h), (IDTB[:], idtb_h),
                         (GWT[:].rearrange("p (k e) -> p k e", k=KT),
                          gw_h.rearrange("k p e -> p k e")),
                         (EPSC[:], eps_h)):
            ev_const = E_gp(lambda q, d=dst, s=src: q.dma_start(out=d, in_=s), "cn")
        ev_state0 = E_dv(lambda q: q.memset(STATE[:], 0.0), mark=True)
        ev_out_all = None

        x_kpt = x_h.rearrange("(k p) t -> p k t", p=P)

        nchunks = T_core // C
        chunks = [("warm", 0, W)] + [("real", W + i * C, C) for i in range(nchunks)]
        ev_y2 = [None] * KT

        for ci, (kind, tok0, Cc) in enumerate(chunks):
            warm = kind == "warm"
            nsz = min(512, Cc)
            nn = Cc // nsz
            jt = Cc // P

            gp.wait(ev_const)
            gp.wait(ev_xb_free)
            gp.wait(ev_out_all)
            ev_xb = E_gp(lambda q, t0=tok0, cc=Cc:
                         q.dma_start(out=XB[:].rearrange("p (k c) -> p k c", k=KT)[:, :, 0:cc],
                                     in_=x_kpt[:, :, t0:t0 + cc]),
                         "xb")

            mats = [("i", wi_h), ("f", wf_h)] if warm else [("i", wi_h), ("f", wf_h), ("g", wg_h)]
            ev_scan = [None] * KT
            ev_sg = [None] * KT
            ev_o2 = [None] * KT
            for half in range(2):
                for mi in range(8):
                    m = half * 8 + mi
                    ev_mm = {}
                    for mat, hsrc in mats:
                        blk, slot = wload(hsrc[m])
                        ps = {"i": PA, "f": PF, "g": PB}[mat]
                        te.wait(wslot_ld[slot])
                        te.wait(ev_xb)
                        te.wait(pfree["PA" if mat == "i" else "PF" if mat == "f" else "PB"])
                        last = mm_block(blk, lambda k: xb_t(k), ps, Cc)
                        wslot_free[slot] = last
                        ev_mm[mat] = last
                    ac.wait(ev_mm["f"])
                    ac.wait(ev_gate_free[mi])
                    ac.wait(ev_gv_free)
                    ac.wait(ev_eph012[0])
                    ac.wait(ev_eph012[1])
                    ev_g1 = E_ac(lambda q, mi=mi, cc=Cc:
                                 q.activation(gate_t(mi)[:, 0:cc], PF[:, 0:cc], AF.Sigmoid),
                                 mark=True)
                    ev_e0 = E_ac(lambda q, cc=Cc:
                                 q.activation(eph_t(0)[:, 0:cc], PF[:, 0:cc], AF.Sigmoid, scale=-1.0),
                                 mark=True)
                    pfree["PF"] = ev_e0
                    ac.wait(ev_mm["i"])
                    ev_e1 = E_ac(lambda q, cc=Cc:
                                 q.activation(eph_t(1)[:, 0:cc], PA[:, 0:cc], AF.Silu),
                                 mark=True)
                    pfree["PA"] = ev_e1
                    dv.wait(ev_e0)
                    dv.wait(ev_e1)
                    dv.wait(ev_t2_dma)
                    ev_v = E_dv(lambda q, mi=mi, cc=Cc:
                                q.tensor_mul(v_t(mi)[:, 0:cc], eph_t(1)[:, 0:cc], eph_t(0)[:, 0:cc]),
                                mark=True)
                    ev_eph012[0] = ev_v
                    ev_eph012[1] = ev_v
                    if not warm:
                        ac.wait(ev_mm["g"])
                        ac.wait(ev_eph012[2])
                        ev_e2 = E_ac(lambda q, cc=Cc:
                                     q.activation(eph_t(2)[:, 0:cc], PB[:, 0:cc], AF.Silu),
                                     mark=True)
                        pfree["PB"] = ev_e2
                        dv.wait(ev_e2)
                        dv.wait(ev_sg_free)
                        dv.wait(ev_hfree[0])
                        ev_sg[m] = E_dv(lambda q, m=m, cc=Cc:
                                        q.tensor_single_scalar(sg_t(m)[:, 0:cc], eph_t(2)[:, 0:cc],
                                                               GN[:, m:m + 1], ALU.mult),
                                        mark=True)
                        ev_eph012[2] = ev_sg[m]
                    dv.wait(ev_g1)
                    dv.wait(ev_state0)
                    dvs(lambda q, m=m, mi=mi, cc=Cc:
                        q.tensor_tensor_scan(o_t(m)[:, 0:cc], gate_t(mi)[:, 0:cc],
                                             v_t(mi)[:, 0:cc], STATE[:, m:m + 1],
                                             ALU.mult, ALU.add))
                    ev_scan[m] = dvs(lambda q, m=m, cc=Cc:
                                     q.tensor_copy(STATE[:, m:m + 1], o_t(m)[:, cc - 1:cc]))
                    ev_gate_free[mi] = ev_scan[m]
                    if m == KT - 1:
                        ev_xb_free = ev_mm[mats[-1][0]]
            if warm:
                if dbg:
                    ev_d = dump(dbg_st, STATE[:], [ev_scan[KT - 1]])
                    dv.wait(ev_d)
                    if dbg == 2:
                        ev_d = dump(dbg_v, T2b[:, 0:KT * C // 2], [ev_scan[KT - 1]])
                        ev_d2 = dump(dbg_o, GVb[:, 0:KT * C], [ev_scan[KT - 1]])
                        ev_d3 = dump(dbg_gate, GV[:, 8 * C:KT * C], [ev_scan[KT - 1]])
                        dv.wait(ev_d)
                        dv.wait(ev_d2)
                        ac.wait(ev_d3)
                        sy.wait(ev_d3)
                continue
            last_real = (ci == len(chunks) - 1)
            if dbg == 2:
                last_real = False
            if dbg and last_real:
                ev_d = dump(dbg_v, T2b[:, 0:KT * C // 2], [ev_scan[KT - 1]])
                ev_d2 = dump(dbg_o, GVb[:, 0:KT * C], [ev_scan[KT - 1]])
                ev_d3 = dump(dbg_gate, GV[:, 8 * C:KT * C], [ev_scan[KT - 1]])
                dv.wait(ev_d)
                dv.wait(ev_d2)
                ac.wait(ev_d3)
                sy.wait(ev_d3)

            if stop_after == "s1":
                continue
            ev_row = None
            for k in range(KT):
                sl = 3 + (k % 2)
                ac.wait(ev_scan[k])
                ac.wait(ev_eph34[k % 2])
                ev_sq = E_ac(lambda q, k=k, sl=sl, cc=Cc:
                             q.activation(eph_t(sl)[:, 0:cc], o_t(k)[:, 0:cc], AF.Square),
                             mark=True)
                te.wait(ev_sq)
                if k == 0:
                    te.wait(pfree["PC"])
                evk = None
                for n in range(nn):
                    r = 0 if n == 0 else 32
                    evk = E_te(lambda q, sl=sl, n=n, r=r, nsz=nsz, st=(k == 0), sp=(k == KT - 1):
                               q.matmul(PC[r:r + 1, 0:nsz], ONB[:, :],
                                        eph_t(sl)[:, n * nsz:(n + 1) * nsz],
                                        start=st, stop=sp),
                               mark=(n == nn - 1))
                ev_eph34[k % 2] = evk
                if k == KT - 1:
                    ev_row = evk
            ev_bcr = None
            ev_pcr = None
            for n in range(nn):
                r = 0 if n == 0 else 32
                dv.wait(ev_row)
                ev_pcr = E_dv(lambda q, r=r, nsz=nsz:
                              q.tensor_copy(AB[0:1, 0:nsz], PC[r:r + 1, 0:nsz]),
                              mark=(n == nn - 1))
                ev_r = E_dv(lambda q, nsz=nsz: q.tensor_scalar_mul(AB[0:1, 0:nsz], AB[0:1, 0:nsz], 1.0 / D),
                            mark=True)
                ac.wait(ev_r)
                E_ac(lambda q, nsz=nsz: q.activation(AB[0:1, 0:nsz], AB[0:1, 0:nsz], AF.Ln, bias=EPSC[0:1, 0:1]))
                ev_rr = E_ac(lambda q, nsz=nsz:
                             q.activation(AB[0:1, 0:nsz], AB[0:1, 0:nsz], AF.Exp, scale=-0.5),
                             mark=True)
                te.wait(ev_rr)
                te.wait(pfree["PD"])
                ev_t = E_te(lambda q, nsz=nsz:
                            q.matmul(PD[:, 0:nsz], ON1[:, :], AB[0:1, 0:nsz],
                                     start=True, stop=True),
                            mark=True)
                dv.wait(ev_t)
                ev_bcr = E_dv(lambda q, n=n, nsz=nsz:
                              q.tensor_copy(BC[:, n * nsz:(n + 1) * nsz], PD[:, 0:nsz]),
                              mark=True)
                pfree["PD"] = ev_bcr
                ac.wait(ev_t)
                dv.wait(ev_t)
            pfree["PC"] = ev_pcr
            for k in range(KT):
                dv.wait(ev_bcr)
                dv.wait(ev_sg[k])
                E_dv(lambda q, k=k, cc=Cc:
                     q.tensor_mul(eph_t(5)[:, 0:cc], o_t(k)[:, 0:cc], BC[:, 0:cc]))
                ev_o2[k] = E_dv(lambda q, k=k, cc=Cc:
                                q.tensor_mul(o2_t(k)[:, 0:cc], eph_t(5)[:, 0:cc], sg_t(k)[:, 0:cc]),
                                mark=True)

            if dbg and last_real:
                ev_d = dump(dbg_o2, T2b[:, 0:KT * C], [e_ for e_ in ev_o2])
                dv.wait(ev_d)
                te.wait(ev_d)
            if stop_after == "s3":
                continue
            ev_y = [None] * KT
            ev_x1 = [None] * KT
            ev_x1b = [None] * KT
            ev_rowS = None
            for k in range(KT):
                sy.wait(ev_o2[k])
            sy.wait(ev_scan[KT - 1])
            ev_x = E_sy(lambda q, t0=tok0, cc=Cc:
                        q.dma_start(out=GV[:].rearrange("p (k c) -> p k c", k=KT)[:, :, 0:cc],
                                    in_=x_kpt[:, :, t0:t0 + cc]),
                        "x")
            for m in range(KT):
                pname = "PA" if m % 2 == 0 else "PF"
                ps = PA if m % 2 == 0 else PF
                blk, slot = wload(wo_h[m])
                te.wait(wslot_ld[slot])
                te.wait(pfree[pname])
                for k in range(KT):
                    te.wait(ev_o2[k])
                last = mm_block(blk, lambda k: o2_t(k), ps, Cc)
                wslot_free[slot] = last
                dv.wait(last)
                dv.wait(ev_x)
                ev_y[m] = E_dv(lambda q, m=m, ps=ps, cc=Cc:
                               q.tensor_add(y_t(m)[:, 0:cc], ps[:, 0:cc], y_t(m)[:, 0:cc]),
                               mark=True)
                pfree[pname] = ev_y[m]
                ac.wait(ev_y[m])
                ac.wait(ev_o2[m])
                ev_yb = E_ac(lambda q, m=m, cc=Cc:
                             q.copy(sg_t(m)[:, 0:cc], y_t(m)[:, 0:cc]),
                             mark=True)
                sl = 3 + (m % 2)
                ac.wait(ev_eph34[m % 2])
                ev_sqy = E_ac(lambda q, m=m, sl=sl, cc=Cc:
                              q.activation(eph_t(sl)[:, 0:cc], y_t(m)[:, 0:cc], AF.Square),
                              mark=True)
                te.wait(ev_yb)
                te.wait(ev_sqy)
                if m == 0:
                    te.wait(pfree["PC"])
                    te.wait(pfree["PD"])
                evk = None
                for n in range(nn):
                    rS = 0 if n == 0 else 32
                    psQ = PC[64:65, 0:nsz] if n == 0 else PD[0:1, 0:nsz]
                    E_te(lambda q, m=m, n=n, rS=rS, nsz=nsz, st=(m == 0), sp=(m == KT - 1):
                         q.matmul(PC[rS:rS + 1, 0:nsz], ONB[:, :],
                                  sg_t(m)[:, n * nsz:(n + 1) * nsz], start=st, stop=sp))
                    evk = E_te(lambda q, sl=sl, n=n, psQ=psQ, nsz=nsz, st=(m == 0), sp=(m == KT - 1):
                               q.matmul(psQ, ONB[:, :],
                                        eph_t(sl)[:, n * nsz:(n + 1) * nsz], start=st, stop=sp),
                               mark=(n == nn - 1))
                ev_eph34[m % 2] = evk
                if m == KT - 1:
                    ev_rowS = evk

            def ln_rows(ev_rows):
                ev_bc = None
                ev_pc_read = None
                for n in reversed(range(nn)):
                    rS = 0 if n == 0 else 32
                    psQ = PC[64:65, 0:nsz] if n == 0 else PD[0:1, 0:nsz]
                    dv.wait(ev_rows)
                    E_dv(lambda q, psQ=psQ, nsz=nsz:
                         q.tensor_copy(AB[0:1, 0:nsz], psQ))
                    ev_pc_read = E_dv(lambda q, rS=rS, nsz=nsz:
                                      q.tensor_copy(AB[0:1, nsz:2 * nsz],
                                                    PC[rS:rS + 1, 0:nsz]),
                                      mark=True)
                    if n == nn - 1 and nn > 1:
                        pfree["PD"] = ev_pc_read
                    E_dv(lambda q, nsz=nsz:
                         q.tensor_scalar_mul(AB[0:1, 0:nsz], AB[0:1, 0:nsz], 1.0 / D))
                    E_dv(lambda q, nsz=nsz:
                         q.tensor_scalar_mul(AB[0:1, nsz:2 * nsz], AB[0:1, nsz:2 * nsz], 1.0 / D))
                    E_dv(lambda q, nsz=nsz:
                         q.tensor_mul(MRow[0:1, 0:nsz],
                                      AB[0:1, nsz:2 * nsz], AB[0:1, nsz:2 * nsz]))
                    ev_var = E_dv(lambda q, nsz=nsz:
                                  q.tensor_sub(AB[0:1, 0:nsz], AB[0:1, 0:nsz],
                                               MRow[0:1, 0:nsz]),
                                  mark=True)
                    ac.wait(ev_var)
                    E_ac(lambda q, nsz=nsz:
                         q.activation(AB[0:1, 0:nsz], AB[0:1, 0:nsz], AF.Ln, bias=EPSC[0:1, 0:1]))
                    ev_A = E_ac(lambda q, nsz=nsz:
                                q.activation(AB[0:1, 0:nsz], AB[0:1, 0:nsz], AF.Exp, scale=-0.5),
                                mark=True)
                    dv.wait(ev_A)
                    E_dv(lambda q, nsz=nsz:
                         q.tensor_mul(AB[0:1, nsz:2 * nsz], AB[0:1, nsz:2 * nsz],
                                      AB[0:1, 0:nsz]))
                    ev_B = E_dv(lambda q, nsz=nsz:
                                q.tensor_scalar_mul(AB[0:1, nsz:2 * nsz],
                                                    AB[0:1, nsz:2 * nsz], -1.0),
                                mark=True)
                    for off_ab, off_bc, evr in ((0, 0, ev_A), (nsz, C, ev_B)):
                        te.wait(evr)
                        te.wait(pfree["PD"])
                        ev_t = E_te(lambda q, off_ab=off_ab, nsz=nsz:
                                    q.matmul(PD[:, 0:nsz], ON1[:, :],
                                             AB[0:1, off_ab:off_ab + nsz],
                                             start=True, stop=True),
                                    mark=True)
                        dv.wait(ev_t)
                        ev_bc = E_dv(lambda q, off_bc=off_bc, n=n, nsz=nsz:
                                     q.tensor_copy(BC[:, off_bc + n * nsz:off_bc + (n + 1) * nsz],
                                                   PD[:, 0:nsz]),
                                     mark=True)
                        pfree["PD"] = ev_bc
                    ac.wait(ev_t)
                    dv.wait(ev_t)
                return ev_bc, ev_pc_read

            if dbg and last_real:
                ev_d = dump(dbg_y, GV[:], [e_ for e_ in ev_y])
                dv.wait(ev_d)
            ev_bc1, ev_pc1 = ln_rows(ev_rowS)
            pfree["PC"] = ev_pc1
            for m in range(KT):
                dv.wait(ev_bc1)
                E_dv(lambda q, m=m, cc=Cc:
                     q.tensor_mul(y_t(m)[:, 0:cc], y_t(m)[:, 0:cc], BC[:, 0:cc]))
                ev_x1[m] = E_dv(lambda q, m=m, cc=Cc:
                                q.tensor_add(y_t(m)[:, 0:cc], y_t(m)[:, 0:cc], BC[:, C:C + cc]),
                                mark=True)
                ac.wait(ev_x1[m])
                ev_x1b[m] = E_ac(lambda q, m=m, cc=Cc:
                                 q.copy(xb_t(m)[:, 0:cc], y_t(m)[:, 0:cc]),
                                 mark=True)

            if dbg and last_real:
                ev_d = dump(dbg_x1, GV[:], [e_ for e_ in ev_x1])
                dv.wait(ev_d)
                te.wait(ev_d)
            if stop_after == "s4":
                continue
            ev_l8 = None
            te.wait(pfree["PC"])
            for n in range(nn):
                for m in range(KT):
                    te.wait(ev_x1b[m])
                rb = 0 if n == 0 else 32
                last = None
                for k in range(KT):
                    last = E_te(lambda q, k=k, n=n, rb=rb, nsz=nsz, st=(k == 0), sp=(k == KT - 1):
                                q.matmul(PC[rb:rb + E, 0:nsz], GWT[:, k * E:(k + 1) * E],
                                         xb_t(k)[:, n * nsz:(n + 1) * nsz],
                                         start=st, stop=sp),
                                mark=(k == KT - 1))
                dv.wait(last)
                ev_l8 = E_dv(lambda q, n=n, rb=rb, nsz=nsz:
                             q.tensor_copy(L8[:, n * nsz:(n + 1) * nsz], PC[rb:rb + E, 0:nsz]),
                             mark=True)
            pfree["PC"] = ev_l8
            if stop_after == "s4b_scores":
                continue
            ev_w8 = None
            for j in range(jt):
                te.wait(ev_l8)
                te.wait(pfree["PD"])
                ev_t = E_te(lambda q, j=j:
                            q.transpose(PDb[:, 0:E], L8[:, j * P:(j + 1) * P], IDTB[0:E, 0:E]),
                            mark=True)
                dv.wait(ev_t)
                ev_lt = dvs(lambda q, j=j:
                            q.tensor_copy(LT[:, j * E:(j + 1) * E], PDb[:, 0:E]))
                pfree["PD"] = ev_lt
                if stop_after == "s4b_t1":
                    continue
                dvs(lambda q, j=j: q.tensor_reduce(M1[:, j:j + 1], LT[:, j * E:(j + 1) * E], AX, ALU.max))
                dvs(lambda q, j=j: q.tensor_scalar_mul(M1N[:, j:j + 1], M1[:, j:j + 1], -1.0))
                dvs(lambda q, j=j: q.tensor_single_scalar(MSK[:, j * E:(j + 1) * E],
                                                          LT[:, j * E:(j + 1) * E], M1[:, j:j + 1], ALU.is_ge))
                dvs(lambda q, j=j: q.scalar_tensor_tensor(MSK[:, j * E:(j + 1) * E],
                                                          MSK[:, j * E:(j + 1) * E], NEG_BIG,
                                                          LT[:, j * E:(j + 1) * E], ALU.mult, ALU.add))
                dvs(lambda q, j=j: q.tensor_reduce(M2[:, j:j + 1], MSK[:, j * E:(j + 1) * E], AX, ALU.max))
                ev_mk = dvs(lambda q, j=j: q.tensor_single_scalar(MSK[:, j * E:(j + 1) * E],
                                                                  LT[:, j * E:(j + 1) * E],
                                                                  M2[:, j:j + 1], ALU.is_ge))
                if stop_after == "s4b_mask":
                    continue
                if dbg and last_real and j == jt - 1:
                    ev_d = dump(dbg_lt, LT[:], [ev_mk])
                    ev_d2 = dump(dbg_et, MSK[:], [ev_mk])
                    ac.wait(ev_d)
                    ac.wait(ev_d2)
                    dv.wait(ev_d)
                    dv.wait(ev_d2)
                ac.wait(ev_mk)
                ev_ex = E_ac(lambda q, j=j:
                             q.activation(ET[:, j * E:(j + 1) * E], LT[:, j * E:(j + 1) * E], AF.Exp,
                                          bias=M1N[:, j:j + 1], accum_out=DEN[:, j:j + 1]),
                             mark=True)
                dv.wait(ev_ex)
                dvs(lambda q, j=j: q.reciprocal(DEN[:, j:j + 1], DEN[:, j:j + 1]))
                dvs(lambda q, j=j: q.tensor_mul(MSK[:, j * E:(j + 1) * E],
                                                MSK[:, j * E:(j + 1) * E], ET[:, j * E:(j + 1) * E]))
                dvs(lambda q, j=j: q.tensor_single_scalar(MSK[:, j * E:(j + 1) * E],
                                                          MSK[:, j * E:(j + 1) * E],
                                                          DEN[:, j:j + 1], ALU.mult))
                ev_c = dvs(lambda q, j=j: q.tensor_copy(LT[:, j * E:(j + 1) * E],
                                                        MSK[:, j * E:(j + 1) * E]))
                te.wait(ev_c)
                te.wait(pfree["PD"])
                ev_t2_ = E_te(lambda q, j=j:
                              q.transpose(PDb[0:E, 0:P], LT[:, j * E:(j + 1) * E], IDTB[:, :]),
                              mark=True)
                dv.wait(ev_t2_)
                ev_w8 = E_dv(lambda q, j=j:
                             q.tensor_copy(W8[:, j * P:(j + 1) * P], PDb[0:E, 0:P]),
                             mark=True)
                pfree["PD"] = ev_w8
                if dbg and last_real and j == jt - 1:
                    dump(dbg_msk, MSK[:], [ev_w8])

            if dbg and last_real:
                ev_d = dump(dbg_w8, W8[:], [ev_w8])
                te.wait(ev_d)
                dv.wait(ev_d)
                dump(dbg_den, DEN[:], [ev_w8])
            if stop_after in ("s4b", "s4b_scores", "s4b_t1", "s4b_mask"):
                continue
            ev_wbc_used = None
            for e in range(E):
                par = e % 2
                h_t = hA_t if par == 0 else hB_t
                ev_bcw = None
                for n in range(nn):
                    te.wait(ev_w8)
                    te.wait(pfree["PD"])
                    te.wait(ev_wbc_used)
                    ev_t = E_te(lambda q, e=e, n=n, nsz=nsz:
                                q.matmul(PD[:, 0:nsz], OH[:, e * P:(e + 1) * P],
                                         W8[:, n * nsz:(n + 1) * nsz], start=True, stop=True),
                                mark=True)
                    dv.wait(ev_t)
                    ev_bcw = E_dv(lambda q, n=n, nsz=nsz:
                                  q.tensor_copy(WBC[:, n * nsz:(n + 1) * nsz], PD[:, 0:nsz]),
                                  mark=True)
                    pfree["PD"] = ev_bcw
                if stop_after in ("s5_wbc", "s5_wbc_nomm", "s5_wbc_noev", "s5_wbc_pb"):
                    break
                ev_h = [None] * FT
                ev_t3 = None
                for mf in range(FT):
                    blk1, s1 = wload(w1_h[e, mf])
                    te.wait(wslot_ld[s1])
                    te.wait(pfree["PA"])
                    last1 = mm_block(blk1, lambda k: xb_t(k), PA, Cc)
                    wslot_free[s1] = last1
                    blk3, s3 = wload(w3_h[e, mf])
                    te.wait(wslot_ld[s3])
                    te.wait(pfree["PF"])
                    last3 = mm_block(blk3, lambda k: xb_t(k), PF, Cc)
                    wslot_free[s3] = last3
                    if e == E - 1 and mf == FT - 1:
                        ev_xb_free = last3
                    ac.wait(last1)
                    ac.wait(ev_hfree[par])
                    if par == 0:
                        ac.wait(ev_sg_free)
                        ac.wait(ev_rowS)
                    ev_s = E_ac(lambda q, mf=mf, h_t=h_t, cc=Cc:
                                q.activation(h_t(mf)[:, 0:cc], PA[:, 0:cc], AF.Silu),
                                mark=True)
                    pfree["PA"] = ev_s
                    dv.wait(last3)
                    dv.wait(ev_bcw)
                    ev_t3 = E_dv(lambda q, cc=Cc:
                                 q.tensor_mul(eph_t(5)[:, 0:cc], PF[:, 0:cc], WBC[:, 0:cc]),
                                 mark=True)
                    pfree["PF"] = ev_t3
                    dv.wait(ev_s)
                    ev_h[mf] = E_dv(lambda q, mf=mf, h_t=h_t, cc=Cc:
                                    q.tensor_mul(h_t(mf)[:, 0:cc], h_t(mf)[:, 0:cc], eph_t(5)[:, 0:cc]),
                                    mark=True)
                ev_wbc_used = ev_t3
                if stop_after == "s5_h":
                    break
                last2 = None
                for m in range(KT):
                    blk2, s2 = wload(w2_h[e, m])
                    te.wait(wslot_ld[s2])
                    te.wait(pfree["PB"])
                    for k in range(KT):
                        te.wait(ev_h[k])
                    last2 = mm_block(blk2, lambda k, h_t=h_t: h_t(k), PB, Cc)
                    wslot_free[s2] = last2
                    dv.wait(last2)
                    dv.wait(ev_x1[m] if e == 0 else ev_y2[m])
                    ev_y2[m] = E_dv(lambda q, m=m, cc=Cc:
                                    q.tensor_add(y_t(m)[:, 0:cc], PB[:, 0:cc], y_t(m)[:, 0:cc]),
                                    mark=True)
                    pfree["PB"] = ev_y2[m]
                ev_hfree[par] = last2
                if stop_after == "s5_e0":
                    break

            if dbg and last_real:
                ev_d = dump(dbg_y2, GV[:], [e_ for e_ in ev_y2])
                ac.wait(ev_d)
            if stop_after == "s5":
                continue
            if stop_after in ("s5", "s5_wbc", "s5_h", "s5_e0",
                              "s5_wbc_nomm", "s5_wbc_noev", "s5_wbc_pb"):
                continue
            ev_rowS2 = None
            for m in range(KT):
                ac.wait(ev_y2[m])
                ac.wait(ev_hfree[0])
                ev_yb2 = E_ac(lambda q, m=m, cc=Cc:
                              q.copy(sg_t(m)[:, 0:cc], y_t(m)[:, 0:cc]),
                              mark=True)
                sl = 3 + (m % 2)
                ac.wait(ev_eph34[m % 2])
                ev_sq2 = E_ac(lambda q, m=m, sl=sl, cc=Cc:
                              q.activation(eph_t(sl)[:, 0:cc], y_t(m)[:, 0:cc], AF.Square),
                              mark=True)
                te.wait(ev_yb2)
                te.wait(ev_sq2)
                if m == 0:
                    te.wait(pfree["PC"])
                    te.wait(pfree["PD"])
                evk = None
                for n in range(nn):
                    rS = 0 if n == 0 else 32
                    psQ = PC[64:65, 0:nsz] if n == 0 else PD[0:1, 0:nsz]
                    E_te(lambda q, m=m, n=n, rS=rS, nsz=nsz, st=(m == 0), sp=(m == KT - 1):
                         q.matmul(PC[rS:rS + 1, 0:nsz], ONB[:, :],
                                  sg_t(m)[:, n * nsz:(n + 1) * nsz], start=st, stop=sp))
                    evk = E_te(lambda q, sl=sl, n=n, psQ=psQ, nsz=nsz, st=(m == 0), sp=(m == KT - 1):
                               q.matmul(psQ, ONB[:, :],
                                        eph_t(sl)[:, n * nsz:(n + 1) * nsz], start=st, stop=sp),
                               mark=(n == nn - 1))
                ev_eph34[m % 2] = evk
                if m == KT - 1:
                    ev_rowS2 = evk
            ev_sg_free_new = ev_rowS2
            ev_bc2, ev_pc2 = ln_rows(ev_rowS2)
            pfree["PC"] = ev_pc2
            ev_out_dma_hi = None
            for m in range(KT):
                dv.wait(ev_bc2)
                E_dv(lambda q, m=m, cc=Cc:
                     q.tensor_mul(out_t(m)[:, 0:cc], y_t(m)[:, 0:cc], BC[:, 0:cc]))
                ev_o = E_dv(lambda q, m=m, cc=Cc:
                            q.tensor_add(out_t(m)[:, 0:cc], out_t(m)[:, 0:cc], BC[:, C:C + cc]),
                            mark=True)
                gp.wait(ev_o)
                ev_dma = E_gp(lambda q, m=m, t0=tok0, cc=Cc:
                              q.dma_start(out=out_h[m * P:(m + 1) * P, t0 - W:t0 - W + cc],
                                          in_=out_t(m)[:, 0:cc]),
                              "out")
                if m >= 8:
                    ev_out_dma_hi = ev_dma
                if m == KT - 1:
                    ev_gv_free = ev_o
            ev_t2_dma = ev_out_dma_hi
            ev_out_all = (sems["out"], cnt["out"])
            ev_sg_free = ev_sg_free_new

        for w_ in ("wt0", "wt1", "xb", "out", "x", "cn"):
            gp.wait((sems[w_], cnt[w_]))
        gp.wait((s_dv, cnt["dv"]))
        gp.wait((s_ac, cnt["ac"]))
        gp.wait((s_pe, cnt["pe"]))

        block.sync(lambda q: sy.replay(q))
        block.gpsimd(lambda q: gp.replay(q))
        block.tensor(lambda q: te.replay(q))
        block.vector(lambda q: dv.replay(q))
        block.scalar(lambda q: ac.replay(q))
    return nc


def _prep_shared(Wi, Wf, Wg, Wo, gnorm, gate_w, w1, w3, w2):
    f32 = lambda a: np.ascontiguousarray(np.asarray(a, dtype=np.float32))
    sh = {}
    sh["wi_blk"] = _wblocks(f32(Wi))
    sh["wf_blk"] = _wblocks(f32(Wf))
    sh["wg_blk"] = _wblocks(f32(Wg))
    sh["wo_blk"] = _wblocks(f32(Wo))
    w1, w3, w2 = f32(w1), f32(w3), f32(w2)
    sh["w1_blk"] = np.ascontiguousarray(np.stack([_wblocks(w1[e]) for e in range(E)]))
    sh["w3_blk"] = np.ascontiguousarray(np.stack([_wblocks(w3[e]) for e in range(E)]))
    sh["w2_blk"] = np.ascontiguousarray(np.stack([_wblocks(w2[e]) for e in range(E)]))
    sh["gw_t"] = np.ascontiguousarray(f32(gate_w).T.reshape(KT, P, E))
    sh["gn_t"] = np.ascontiguousarray(f32(gnorm).reshape(KT, P).T)
    sh["idt"] = np.eye(P, dtype=np.float32)
    sh["idtb"] = np.eye(P, dtype=np.float32)
    oh = np.zeros((E, E * P), np.float32)
    for e in range(E):
        oh[e, e * P:(e + 1) * P] = 1.0
    sh["oh"] = oh
    sh["on1"] = np.ones((1, P), np.float32)
    sh["epsc"] = np.full((1, 1), 1e-5, np.float32)
    sh["onb"] = np.ones((P, 1), np.float32)
    return sh


def run_module(x, Wi, Wf, Wg, Wo, gnorm, gate_w, w1, w3, w2, W=512, C=1024, dbg=False, stop_after=None):
    x = np.asarray(x, dtype=np.float32)
    B, T, _ = x.shape
    per_b = NCORES // B
    T_core = T // per_b
    sh = _prep_shared(Wi, Wf, Wg, Wo, gnorm, gate_w, w1, w3, w2)
    ims = []
    for cid in range(NCORES):
        b, q = cid // per_b, cid % per_b
        t0 = q * T_core
        warm = np.zeros((W, D), np.float32) if t0 == 0 else x[b, t0 - W:t0]
        xc = np.concatenate([warm, x[b, t0:t0 + T_core]], axis=0)
        im = dict(sh)
        im["x_fm"] = np.ascontiguousarray(xc.T)
        ims.append(im)
    nc = build_nc(T_core, W, C, dbg=dbg, stop_after=stop_after)
    res = run_bass_kernel_spmd(nc, ims, core_ids=list(range(NCORES)))
    out = np.empty((B, T, D), np.float32)
    for cid in range(NCORES):
        b, q = cid // per_b, cid % per_b
        out[b, q * T_core:(q + 1) * T_core] = np.asarray(res.results[cid]["out_fm"]).T
    if dbg:
        return out, res.results
    return out


def kernel(L0_attn_hidden_states, L0_attn_Wi, L0_attn_Wf, L0_attn_Wg, L0_attn_Wo,
           L0_attn_gnorm_w, L0_ffn_gate_w, L0_ffn_w1, L0_ffn_w3, L0_ffn_w2):
    return run_module(L0_attn_hidden_states, L0_attn_Wi, L0_attn_Wf, L0_attn_Wg,
                      L0_attn_Wo, L0_attn_gnorm_w, L0_ffn_gate_w, L0_ffn_w1,
                      L0_ffn_w3, L0_ffn_w2, W=512, C=1024)
